# version 15
# baseline (speedup 1.0000x reference)
import contextlib
import numpy as np
import concourse.bass as bass
import concourse.mybir as mybir

F32 = mybir.dt.float32
BF16 = mybir.dt.bfloat16
AF = mybir.ActivationFunctionType
ALU = mybir.AluOpType
AX = mybir.AxisListType

D = 1024
H = 16
DK = 64
C = 128
SC = 512
NE = 32
EI = 512
SI = 2048
LN_EPS = 1e-5
GT = 1024
ROUTED_SCALE = 2.5
L = 2048

ENGINES = ("sync", "tensor", "vector", "scalar", "gpsimd")


def host_consts():
    import ml_dtypes
    t = np.arange(C, dtype=np.float32)
    deltaT = np.maximum(t[None, :] - t[:, None], 0.0).astype(np.float32)
    causalT = (t[None, :] >= t[:, None]).astype(np.float32)
    tmod1 = np.broadcast_to(t + 1.0, (C, C)).astype(np.float32).copy()
    revcol = (C - 1 - t).reshape(C, 1).astype(np.float32)
    ident_bf = np.eye(C).astype(ml_dtypes.bfloat16)
    return dict(c_deltaT=deltaT, c_causalT=causalT, c_tmod1=tmod1,
                c_revcol=revcol, c_ident=ident_bf)


class Ctr:
    def __init__(self, sem, step=1):
        self.sem = sem
        self.n = 0
        self.step = step


class Rec:
    def __init__(self):
        self.ops = {e: [] for e in ENGINES}

    def op(self, eng, fn, waits=(), inc=None, selfsync=None):
        wl = []
        for (c, v) in waits:
            assert v <= c.n, f"wait on future value {v} > {c.n}"
            if v > 0:
                wl.append((c.sem, v))
        if selfsync is None:
            selfsync = eng in ("vector", "scalar")
        post = None
        if inc is not None:
            if selfsync and inc.n > 0:
                wl.append((inc.sem, inc.n))
            inc.n += inc.step
            post = inc.n
        self.ops[eng].append((wl, fn, (inc.sem, inc.step) if inc else None))
        return post

    def flush(self, block):
        for eng in ENGINES:
            entries = self.ops[eng]
            if not entries:
                continue

            def make(entries):
                def run(e):
                    for (wl, fn, inc) in entries:
                        for (sem, v) in wl:
                            e.wait_ge(sem, v)
                        inst = fn(e)
                        if inc is not None:
                            assert inst is not None
                            inst.then_inc(inc[0], inc[1])
                return run
            getattr(block, eng)(make(entries))


def build(n_batches=4, debug=False, do_attn=True, do_ffn=True,
          n_sc_limit=None, n_grp_limit=None, attn_stage=99):
    T_core = n_batches * L
    NSC = (T_core // SC) if n_sc_limit is None else n_sc_limit
    NGRP = (T_core // GT) if n_grp_limit is None else n_grp_limit
    nc = bass.Bass("TRN2", target_bir_lowering=False)

    x_in = nc.dram_tensor("x", [n_batches, L, D], F32, kind="ExternalInput").ap()
    w_qkv = nc.dram_tensor("w_qkv", [3 * D, D], F32, kind="ExternalInput").ap()
    w_o = nc.dram_tensor("w_o", [D, D], F32, kind="ExternalInput").ap()
    log_gamma = nc.dram_tensor("log_gamma", [1, H], F32, kind="ExternalInput").ap()
    gate_w = nc.dram_tensor("gate_w", [NE, D], F32, kind="ExternalInput").ap()
    e_bias = nc.dram_tensor("e_bias", [1, NE], F32, kind="ExternalInput").ap()
    up_w = nc.dram_tensor("up_w", [NE, EI, D], F32, kind="ExternalInput").ap()
    down_w = nc.dram_tensor("down_w", [NE, D, EI], F32, kind="ExternalInput").ap()
    shup_w = nc.dram_tensor("shup_w", [SI, D], F32, kind="ExternalInput").ap()
    shdown_w = nc.dram_tensor("shdown_w", [D, SI], F32, kind="ExternalInput").ap()
    hc = host_consts()
    cst = {}
    for n, a in hc.items():
        dt = BF16 if a.dtype.name == "bfloat16" else F32
        cst[n] = nc.dram_tensor(n, list(a.shape), dt, kind="ExternalInput").ap()

    out = nc.dram_tensor("out", [n_batches, L, D], F32, kind="ExternalOutput").ap()

    wqkv_bf = nc.dram_tensor("wqkv_bf", [3 * D, D], BF16, kind="Internal").ap()
    wo_bf = nc.dram_tensor("wo_bf", [D, D], BF16, kind="Internal").ap()
    gate_bf = nc.dram_tensor("gate_bf", [NE, D], BF16, kind="Internal").ap()
    up_bf = nc.dram_tensor("up_bf", [NE, EI, D], BF16, kind="Internal").ap()
    down_bf = nc.dram_tensor("down_bf", [NE, D, EI], BF16, kind="Internal").ap()
    shup_bf = nc.dram_tensor("shup_bf", [SI, D], BF16, kind="Internal").ap()
    shdown_bf = nc.dram_tensor("shdown_bf", [D, SI], BF16, kind="Internal").ap()
    dkind = "ExternalOutput" if debug else "Internal"
    xn_dram = nc.dram_tensor("xn_scratch", [T_core, D], BF16, kind=dkind).ap()
    s_dram = nc.dram_tensor("s_scratch", [T_core, D], BF16, kind=dkind).ap()

    ostack = contextlib.ExitStack()
    with ostack:
        def mk_alloc(stk):
            def sb(name, shape, dt=F32):
                return stk.enter_context(nc.sbuf_tensor(name, shape, dt))
            def psb(name, shape, dt=F32):
                return stk.enter_context(nc.psum_tensor(name, shape, dt))
            return sb, psb
        sb, psb = mk_alloc(ostack)
        mksem = lambda name: ostack.enter_context(nc.semaphore(name))

        r = Rec()
        ct = Ctr(mksem("ct"))
        cv = Ctr(mksem("cv"))
        cs = Ctr(mksem("cs"))
        r_cg = Ctr(mksem("cg"))
        ldw0 = Ctr(mksem("ldw0"), 16)
        ldw0b = Ctr(mksem("ldw0b"), 16)
        ldw1 = Ctr(mksem("ldw1"), 16)
        lcst = Ctr(mksem("lcst"), 16)
        lwT = Ctr(mksem("lwT"), 16)
        lxA = [Ctr(mksem("lxA0"), 16), Ctr(mksem("lxA1"), 16)]
        lxnT = [Ctr(mksem("lxnT0"), 16), Ctr(mksem("lxnT1"), 16)]
        sxn = Ctr(mksem("sxn"), 16)
        ss = Ctr(mksem("ss"), 16)
        lgT = Ctr(mksem("lgT"), 16)
        lxg = [Ctr(mksem("lxg0"), 16), Ctr(mksem("lxg1"), 16)]
        lwe = [Ctr(mksem("lwe0"), 16), Ctr(mksem("lwe1"), 16)]
        lxl = [Ctr(mksem("lxl0"), 16), Ctr(mksem("lxl1"), 16)]
        lsl = [Ctr(mksem("lsl0"), 16), Ctr(mksem("lsl1"), 16)]
        sto = [Ctr(mksem("sto0"), 16), Ctr(mksem("sto1"), 16)]

        ones_row = sb("ones_row", [1, 128])
        lgb = sb("lgb", [128, H])
        cstage = [sb(f"cstage{i}", [128, 2048]) for i in range(2)]
        cbf = [sb(f"cbf{i}", [128, 2048], BF16) for i in range(2)]

        cast_jobs = []

        def cast_dma(dst, srcw, ctr):
            cast_jobs.append((dst, srcw, ctr))

        cast_dma(wqkv_bf[:, 0:512], w_qkv[:, 0:512], ldw0)
        cast_dma(wqkv_bf[:, 512:1024], w_qkv[:, 512:1024], ldw0b)
        cast_dma(wo_bf, w_o, ldw0)
        cast_dma(gate_bf, gate_w, ldw1)
        cast_dma(shup_bf, shup_w, ldw1)
        cast_dma(shdown_bf, shdown_w, ldw1)
        for e_ in range(NE):
            cast_dma(up_bf[e_], up_w[e_], ldw1)
            cast_dma(down_bf[e_], down_w[e_], ldw1)

        pf = Ctr(mksem("pf"), 16)
        for b_ in range(n_batches):
            r.op("gpsimd", lambda e, b_=b_: e.dma_start(out=out[b_], in_=x_in[b_]),
                 inc=pf, selfsync=False)

        astack = contextlib.ExitStack()
        sbA, psA = mk_alloc(astack)
        ident_sb = sbA("ident_sb", [128, 128], BF16)
        DT_all = sbA("DT_all", [128, H * C])
        a_sm = sbA("a_sm", [128, 8 * C], BF16)
        b_full = sbA("b_full", [128, D])
        gI_z = sbA("gI_z", [128, H * DK], BF16)
        pstack = contextlib.ExitStack()
        sbP, _psP = mk_alloc(pstack)
        deltaT_sb = sbP("deltaT_sb", [128, 128])
        causalT_sb = sbP("causalT_sb", [128, 128])
        tmod1_sb = sbP("tmod1_sb", [128, 128])
        revcol_sb = sbP("revcol_sb", [128, 1])
        lgrow = sbP("lgrow", [1, H])
        lgtmp = sbP("lgtmp", [1, H])
        atmp = sbP("atmp", [128, 8 * C])
        btmp = sbP("btmp", [128, D])
        gcs = sbP("gcs", [128, H])
        lgQ = sbP("lgQ", [128, 8])

        dstm = dict(c_deltaT=deltaT_sb, c_causalT=causalT_sb, c_tmod1=tmod1_sb,
                    c_revcol=revcol_sb, c_ident=ident_sb)
        for n in dstm:
            r.op("sync", lambda e, d=dstm[n], s=cst[n]: e.dma_start(out=d[:], in_=s),
                 inc=lcst, selfsync=False)
        r.op("sync", lambda e: e.dma_start(out=lgrow[:], in_=log_gamma),
             inc=lcst, selfsync=False)
        CST = lambda: (lcst, lcst.n)

        lcw = [Ctr(mksem("lcw0"), 16), Ctr(mksem("lcw1"), 16)]
        stw_slot = [None, None]
        ccp = [0, 0]

        def emit_casts():
            ci = 0
            for (dstw, srcw, ctr) in cast_jobs:
                R, W = srcw.shape
                rows = (2048 * 128 // W) if W <= 2048 else 0
                assert rows > 0 and 2048 % W == 0 and dstw.shape == (R, W)
                r0 = 0
                while r0 < R:
                    nr = min(rows, R - r0)
                    sl = ci % 2
                    if nr % 128 == 0:
                        cc = nr // 128
                        sv = srcw[r0:r0 + nr, :].rearrange("(c p) w -> p c w", p=128)
                        dv = dstw[r0:r0 + nr, :].rearrange("(c p) w -> p c w", p=128)
                        pshape, fw = 128, cc * W
                        stg = cstage[sl][0:128, 0:fw].rearrange("p (c w) -> p c w", w=W)
                        sbf = cbf[sl][0:128, 0:fw].rearrange("p (c w) -> p c w", w=W)
                    else:
                        sv = srcw[r0:r0 + nr, :]
                        dv = dstw[r0:r0 + nr, :]
                        pshape, fw = nr, W
                        stg = cstage[sl][0:pshape, 0:fw]
                        sbf = cbf[sl][0:pshape, 0:fw]
                    r.op("sync", lambda e, sv=sv, stg=stg: e.dma_start(
                        out=stg, in_=sv),
                        waits=[(cs, ccp[sl])], inc=lcw[sl], selfsync=False)
                    wl = [(lcw[sl], lcw[sl].n)]
                    if stw_slot[sl] is not None:
                        wl.append((stw_slot[sl], stw_slot[sl].n))
                    ccp[sl] = r.op("scalar", lambda e, sl=sl, pshape=pshape, fw=fw: e.activation(
                        out=cbf[sl][0:pshape, 0:fw], in_=cstage[sl][0:pshape, 0:fw],
                        func=AF.Copy),
                        waits=wl, inc=cs)
                    r.op("sync", lambda e, dv=dv, sbf=sbf: e.dma_start(
                        out=dv, in_=sbf),
                        waits=[(cs, ccp[sl])], inc=ctr, selfsync=False)
                    stw_slot[sl] = ctr
                    ci += 1
                    r0 += nr

        v_ones = r.op("vector", lambda e: e.memset(ones_row[:], 1.0), inc=cv)
        s1 = r.op("scalar", lambda e: e.activation(out=lgtmp[:], in_=lgrow[:], func=AF.Exp,
                                                   scale=-1.0), waits=[CST()], inc=cs)
        v1 = r.op("vector", lambda e: e.tensor_scalar(out=lgtmp[:], in0=lgtmp[:], scalar1=1e30,
                                                      scalar2=None, op0=ALU.min),
                  waits=[(cs, s1)], inc=cv)
        s2 = r.op("scalar", lambda e: e.activation(out=lgtmp[:], in_=lgtmp[:], func=AF.Ln,
                                                   bias=1.0), waits=[(cv, v1)], inc=cs)
        v2 = r.op("vector", lambda e: e.tensor_scalar(out=lgrow[:], in0=lgtmp[:], scalar1=-1.0,
                                                      scalar2=None, op0=ALU.mult),
                  waits=[(cs, s2)], inc=cv)
        with nc.psum_tensor("lg_ps", [128, 512], F32) as lg_ps_t:
            lg_ps = lg_ps_t.ap()
            t1 = r.op("tensor", lambda e: e.matmul(lg_ps[:, 0:H], ones_row[:], lgrow[:],
                                                   start=True, stop=True),
                      waits=[(cv, v2), (cv, v_ones)], inc=ct)
            v_lgb = r.op("vector", lambda e: e.tensor_copy(out=lgb[:], in_=lg_ps[:, 0:H]),
                         waits=[(ct, t1)], inc=cv)
        v4 = r.op("vector", lambda e: e.tensor_scalar(
            out=btmp[:].rearrange("p (h d) -> p h d", d=DK),
            in0=lgb[:].unsqueeze(2).broadcast_to([128, H, DK]),
            scalar1=revcol_sb[:, 0:1], scalar2=None, op0=ALU.mult),
            waits=[(cv, v_lgb), CST()], inc=cv)
        r.op("scalar", lambda e: e.activation(out=b_full[:], in_=btmp[:], func=AF.Exp),
             waits=[(cv, v4)], inc=cs)
        v5 = r.op("vector", lambda e: e.tensor_scalar(out=gcs[:], in0=lgb[:], scalar1=float(C),
                                                      scalar2=None, op0=ALU.mult),
                  waits=[(cv, v_lgb)], inc=cv)
        sgc = r.op("scalar", lambda e: e.activation(out=gcs[:], in_=gcs[:], func=AF.Exp),
                   waits=[(cv, v5)], inc=cs)
        r.op("vector", lambda e: e.memset(gI_z[:], 0.0), inc=cv)
        for pb in range(2):
            r.op("vector", lambda e, pb=pb: e.tensor_tensor(
                out=gI_z[pb * 64:(pb + 1) * 64, :]
                    .rearrange("p (j two d) -> p j two d", two=2, d=DK)
                    [:, :, pb:pb + 1, :].squeeze(2),
                in0=ident_sb[pb * 64:(pb + 1) * 64, pb * 64:(pb + 1) * 64]
                    .unsqueeze(1).broadcast_to([64, 8, DK]),
                in1=gcs[pb * 64:(pb + 1) * 64, pb:H:2].unsqueeze(2).broadcast_to([64, 8, DK]),
                op=ALU.mult),
                waits=[(cs, sgc), CST()], inc=cv)
            r.op("vector", lambda e, pb=pb: e.tensor_copy(
                out=lgQ[pb * 64:(pb + 1) * 64, :],
                in_=lgb[pb * 64:(pb + 1) * 64, pb:H:2]),
                waits=[(cv, v_lgb)], inc=cv)
        v_lgq = cv.n
        vd1 = r.op("vector", lambda e: e.tensor_tensor(
            out=DT_all[:].rearrange("p (h t) -> p h t", t=C),
            in0=deltaT_sb[:].unsqueeze(1).broadcast_to([128, H, C]),
            in1=lgb[:].unsqueeze(2).broadcast_to([128, H, C]), op=ALU.mult),
            waits=[(cv, v_lgb), CST()], inc=cv)
        sd1 = r.op("scalar", lambda e: e.activation(out=DT_all[:], in_=DT_all[:], func=AF.Exp),
                   waits=[(cv, vd1)], inc=cs)
        r.op("vector", lambda e: e.tensor_tensor(
            out=DT_all[:].rearrange("p (h t) -> p h t", t=C),
            in0=DT_all[:].rearrange("p (h t) -> p h t", t=C),
            in1=causalT_sb[:].unsqueeze(1).broadcast_to([128, H, C]), op=ALU.mult),
            waits=[(cs, sd1), CST()], inc=cv)
        va = r.op("vector", lambda e: e.tensor_tensor(
            out=atmp[:].rearrange("p (f t) -> p f t", t=C),
            in0=tmod1_sb[:].unsqueeze(1).broadcast_to([128, 8, C]),
            in1=lgQ[:].unsqueeze(2).broadcast_to([128, 8, C]), op=ALU.mult),
            waits=[(cv, v_lgq), CST()], inc=cv)
        r.op("scalar", lambda e: e.activation(out=a_sm[:], in_=atmp[:], func=AF.Exp),
             waits=[(cv, va)], inc=cs)
        decay_s = cs.n
        decay_v = cv.n
        emit_casts()
        pstack.close()

        with astack:
            wqkvT = sbA("wqkvT", [128, 8 * 3 * D], BF16)
            woT = sbA("woT", [128, 8 * D], BF16)
            if do_attn:
                x_tc = [sbA(f"x_tc{i}", [128, D]) for i in range(2)]
                xn_sc = sbA("xn_sc", [128, 4 * D], BF16)
                xnT_sc = [sbA(f"xnT_sc{i}", [128, 8 * SC], BF16) for i in range(2)]
                s_sb = sbA("s_sb", [128, 4 * D], BF16)
                stat = sbA("stat", [128, 8])
                sqtmp = sbA("sqtmp", [128, D])
                kstat = sbA("kstat", [128, 64])
                QT = sbA("QT", [128, 8 * SC], BF16)
                Qa = sbA("Qa", [128, 8 * SC], BF16)
                Kf = sbA("Kf", [128, D])
                Kbf = sbA("Kbf", [128, 4 * D], BF16)
                Vbf = sbA("Vbf", [128, 4 * D], BF16)
                Vw_ch = sbA("Vw_ch", [128, D], BF16)
                KT_z = sbA("KT_z", [128, H * C], BF16)
                A_bf = sbA("A_bf", [128, H * C], BF16)
                yT = sbA("yT", [128, 8 * SC], BF16)
                S_z = sbA("S_z", [128, H * DK], BF16)

                qkv_ps = [psA(f"qkv_ps{i}", [128, 512]) for i in range(2)]
                kt_ps = psA("kt_ps", [128, 1024], BF16)
                sc_ps = psA("sc_ps", [128, 8 * C])
                y_ps = psA("y_ps", [128, 8 * C])
                st_ps = psA("st_ps", [128, 8 * DK])

                free_x = [(cv, 0), (cv, 0)]
                free_xnT = [(ct, 0), (ct, 0)]
                ev_qkv = [[], []]
                st_xn_prev = 0
                st_s_prev = 0
                last_state_v = 0
                last_yevac_s = 0
                last_kt_v = 0
                last_A_v = 0
                last_Y_t = 0
                last_ksq_v = 0
                last_kln_v = 0
                v_xn_prev = 0
                ev_s = 0
                vxn_tc = [0, 0, 0, 0]
                first_A_mm = True

                r.op("vector", lambda e: e.memset(KT_z[:], 0.0), inc=cv)
                lnout = {}

                def emit_ln_tc(isc, tc):
                    nonlocal st_xn_prev, last_ksq_v, v_xn_prev
                    b, sc = divmod(isc, L // SC)
                    if True:
                        xb = (isc * 4 + tc) % 2
                        xa = x_in[b, sc * SC + tc * 128: sc * SC + (tc + 1) * 128, :]
                        r.op("sync", lambda e, xa=xa, xb=xb: e.dma_start(
                            out=x_tc[xb][:], in_=xa), waits=[free_x[xb]], inc=lxA[xb],
                            selfsync=False)
                        ld_x = (lxA[xb], lxA[xb].n)
                        s_sq = r.op("scalar", lambda e, xb=xb: e.activation(
                            out=sqtmp[:], in_=x_tc[xb][:], func=AF.Square,
                            accum_out=stat[:, 1:2]),
                            waits=[ld_x, (cv, last_ksq_v), (cv, v_xn_prev)], inc=cs)
                        s_sum = r.op("scalar", lambda e, xb=xb: e.activation(
                            out=sqtmp[:], in_=x_tc[xb][:], func=AF.Identity,
                            accum_out=stat[:, 0:1]),
                            waits=[], inc=cs)
                        r.op("scalar", lambda e: e.activation(
                            out=stat[:, 2:3], in_=stat[:, 0:1], func=AF.Identity,
                            scale=1.0 / D), inc=cs)
                        r.op("scalar", lambda e: e.activation(
                            out=stat[:, 3:4], in_=stat[:, 2:3], func=AF.Square), inc=cs)
                        s_va = r.op("scalar", lambda e: e.activation(
                            out=stat[:, 1:2], in_=stat[:, 1:2], func=AF.Identity,
                            scale=1.0 / D), inc=cs)
                        v_v2 = r.op("vector", lambda e: e.tensor_scalar(
                            out=stat[:, 1:2], in0=stat[:, 1:2], scalar1=stat[:, 3:4],
                            scalar2=LN_EPS, op0=ALU.subtract, op1=ALU.add),
                            waits=[(cs, s_va)], inc=cv)
                        s_std = r.op("scalar", lambda e: e.activation(
                            out=stat[:, 4:5], in_=stat[:, 1:2], func=AF.Sqrt),
                            waits=[(cv, v_v2)], inc=cs)
                        r.op("vector", lambda e: e.reciprocal(
                            out=stat[:, 5:6], in_=stat[:, 4:5]), waits=[(cs, s_std)], inc=cv)
                        wv = [(sxn, st_xn_prev)] if tc == 0 else []
                        v_xn = r.op("vector", lambda e, tc=tc, xb=xb: e.tensor_scalar(
                            out=xn_sc[:, tc * D:(tc + 1) * D], in0=x_tc[xb][:],
                            scalar1=stat[:, 2:3], scalar2=stat[:, 5:6],
                            op0=ALU.subtract, op1=ALU.mult), waits=wv, inc=cv)
                        free_x[xb] = (cv, v_xn)
                        v_xn_prev = v_xn
                        vxn_tc[tc] = v_xn

                def emit_ln_finish(isc):
                    nonlocal st_xn_prev
                    t0 = isc * SC
                    buf = isc % 2
                    for tc in range(4):
                        st_xn_prev = r.op("sync", lambda e, tc=tc, t0=t0: e.dma_start(
                            out=xn_dram[t0 + tc * 128: t0 + (tc + 1) * 128, :],
                            in_=xn_sc[:, tc * D:(tc + 1) * D]),
                            waits=[(cv, vxn_tc[tc])], inc=sxn, selfsync=False)
                    st_xn = st_xn_prev
                    for dc in range(8):
                        r.op("sync", lambda e, dc=dc, buf=buf, t0=t0: e.dma_start(
                            out=xnT_sc[buf][:, dc * SC:(dc + 1) * SC],
                            in_=xn_dram[t0:t0 + SC, dc * 128:(dc + 1) * 128], transpose=True),
                            waits=[(sxn, st_xn), free_xnT[buf]], inc=lxnT[buf], selfsync=False)
                    lnout[isc] = (lxnT[buf], lxnT[buf].n)

                def emit_ln(isc):
                    for tc in range(4):
                        emit_ln_tc(isc, tc)
                    emit_ln_finish(isc)

                emit_ln(0)
                ld_wqkvT_list = []
                for dc in range(8):
                    wc = (ldw0, ldw0.n) if dc < 4 else (ldw0b, ldw0b.n)
                    r.op("sync", lambda e, dc=dc: e.dma_start(
                        out=wqkvT[:, dc * 3 * D:(dc + 1) * 3 * D],
                        in_=wqkv_bf[:, dc * 128:(dc + 1) * 128], transpose=True),
                        waits=[wc] +
                              ([(cv, decay_v), (cs, decay_s)] if dc == 0 else []),
                        inc=lwT, selfsync=False)
                for dc in range(8):
                    r.op("sync", lambda e, dc=dc: e.dma_start(
                        out=woT[:, dc * D:(dc + 1) * D],
                        in_=wo_bf[:, dc * 128:(dc + 1) * 128], transpose=True),
                        waits=[(ldw0, ldw0.n)], inc=lwT, selfsync=False)
                WT = (lwT, lwT.n)

                for isc in range(NSC):
                    b, sc = divmod(isc, L // SC)
                    t0 = isc * SC
                    buf = isc % 2
                    if sc == 0:
                        last_state_v = r.op("vector", lambda e: e.memset(S_z[:], 0.0),
                                            waits=[(ct, last_Y_t)], inc=cv)
                    ld_xnT = lnout[isc]

                    if attn_stage < 2:
                        continue
                    ev_q = [0] * 8
                    for fc in range(8):
                        pp = fc % 2
                        w0 = [ld_xnT, WT, (cv, v_lgb)] + ev_qkv[pp]
                        for dc in range(8):
                            tq = r.op("tensor", lambda e, fc=fc, dc=dc, buf=buf, pp=pp: e.matmul(
                                qkv_ps[pp][:],
                                wqkvT[:, dc * 3 * D + fc * 128: dc * 3 * D + (fc + 1) * 128],
                                xnT_sc[buf][:, dc * SC:(dc + 1) * SC],
                                start=(dc == 0), stop=(dc == 7)),
                                waits=(w0 if dc == 0 else []),
                                inc=(ct if dc == 7 else None), selfsync=False)
                        ev_q[fc] = r.op("scalar", lambda e, fc=fc, pp=pp: e.activation(
                            out=QT[:, fc * SC:(fc + 1) * SC], in_=qkv_ps[pp][:], func=AF.Copy),
                            waits=[(ct, tq)], inc=cs)
                        ev_qkv[pp] = [(cs, ev_q[fc])]
                    v_qa = r.op("vector", lambda e: e.tensor_tensor(
                        out=Qa[:].rearrange("p (f c t) -> p f c t", c=4, t=C),
                        in0=QT[:].rearrange("p (f c t) -> p f c t", c=4, t=C),
                        in1=a_sm[:].rearrange("p (f t) -> p f t", t=C).unsqueeze(2)
                            .broadcast_to([128, 8, 4, C]), op=ALU.mult),
                        waits=[(cs, ev_q[7]), (cs, decay_s)], inc=cv)
                    ev_v_tc = [0] * 4
                    v_kln_tc = [0] * 4
                    for tc in range(4):
                        for nch in range(4):
                            pp = nch % 2
                            w0 = list(ev_qkv[pp])
                            for dc in range(8):
                                tkv = r.op("tensor", lambda e, tc=tc, nch=nch, dc=dc, buf=buf,
                                           pp=pp: e.matmul(
                                    qkv_ps[pp][:],
                                    xnT_sc[buf][:, dc * SC + tc * 128: dc * SC + (tc + 1) * 128],
                                    wqkvT[:, dc * 3 * D + D + nch * 512:
                                          dc * 3 * D + D + (nch + 1) * 512],
                                    start=(dc == 0), stop=(dc == 7)),
                                    waits=(w0 if dc == 0 else []),
                                    inc=(ct if dc == 7 else None), selfsync=False)
                            if nch < 2:
                                ev = r.op("scalar", lambda e, nch=nch, pp=pp: e.activation(
                                    out=Kf[:, nch * 512:(nch + 1) * 512], in_=qkv_ps[pp][:],
                                    func=AF.Copy),
                                    waits=[(ct, tkv), (cv, last_kln_v)], inc=cs)
                            else:
                                ev = r.op("scalar", lambda e, tc=tc, nch=nch, pp=pp: e.activation(
                                    out=Vbf[:, tc * D + (nch - 2) * 512:
                                            tc * D + (nch - 1) * 512],
                                    in_=qkv_ps[pp][:], func=AF.Copy),
                                    waits=[(ct, tkv)], inc=cs)
                                ev_v_tc[tc] = ev
                            ev_qkv[pp] = [(cs, ev)]
                        kvw = Kf[:].rearrange("p (h d) -> p h d", d=DK)
                        r.op("vector", lambda e, kvw=kvw: e.reduce_sum(
                            out=kstat[:, 0:16], in_=kvw, axis=AX.X),
                            waits=[(cs, cs.n)], inc=cv)
                        s_ksq = r.op("scalar", lambda e: e.activation(
                            out=sqtmp[:], in_=Kf[:], func=AF.Square),
                            waits=[(cv, last_ksq_v)], inc=cs)
                        last_ksq_v = r.op("vector", lambda e: e.reduce_sum(
                            out=kstat[:, 16:32], in_=sqtmp[:].rearrange("p (h d) -> p h d", d=DK),
                            axis=AX.X), waits=[(cs, s_ksq)], inc=cv)
                        r.op("vector", lambda e: e.tensor_scalar(
                            out=kstat[:, 32:48], in0=kstat[:, 0:16], scalar1=1.0 / DK,
                            scalar2=None, op0=ALU.mult), inc=cv)
                        r.op("vector", lambda e: e.tensor_tensor(
                            out=kstat[:, 0:16], in0=kstat[:, 32:48], in1=kstat[:, 32:48],
                            op=ALU.mult), inc=cv)
                        r.op("vector", lambda e: e.tensor_scalar(
                            out=kstat[:, 16:32], in0=kstat[:, 16:32], scalar1=1.0 / DK,
                            scalar2=LN_EPS, op0=ALU.mult, op1=ALU.add), inc=cv)
                        v_kv = r.op("vector", lambda e: e.tensor_tensor(
                            out=kstat[:, 16:32], in0=kstat[:, 16:32], in1=kstat[:, 0:16],
                            op=ALU.subtract), inc=cv)
                        s_kstd = r.op("scalar", lambda e: e.activation(
                            out=kstat[:, 16:32], in_=kstat[:, 16:32], func=AF.Sqrt),
                            waits=[(cv, v_kv)], inc=cs)
                        r.op("vector", lambda e: e.reciprocal(
                            out=kstat[:, 48:64], in_=kstat[:, 16:32]),
                            waits=[(cs, s_kstd)], inc=cv)
                        r.op("vector", lambda e, kvw=kvw: e.tensor_tensor(
                            out=kvw, in0=kvw,
                            in1=kstat[:, 32:48].unsqueeze(2).broadcast_to([128, H, DK]),
                            op=ALU.subtract), inc=cv)
                        v_kln_tc[tc] = r.op("vector", lambda e, tc=tc, kvw=kvw: e.tensor_tensor(
                            out=Kbf[:, tc * D:(tc + 1) * D].rearrange("p (h d) -> p h d", d=DK),
                            in0=kvw, in1=kstat[:, 48:64].unsqueeze(2).broadcast_to([128, H, DK]),
                            op=ALU.mult), inc=cv)
                        last_kln_v = v_kln_tc[tc]

                    if attn_stage < 3:
                        continue
                    for c in range(4):
                      if True:
                        tkt = 0
                        for f in range(8):
                            tkt = r.op("tensor", lambda e, c=c, f=f: e.transpose(
                                kt_ps[:, f * 128:(f + 1) * 128],
                                Kbf[:, c * D + f * 128: c * D + (f + 1) * 128], ident_sb[:]),
                                waits=([(cv, v_kln_tc[c]), (cv, last_kt_v),
                                        CST()] if f == 0 else []),
                                inc=(ct if f == 7 else None), selfsync=False)
                        for pb_ in range(2):
                            last_kt_v = r.op("vector", lambda e, pb_=pb_: e.tensor_copy(
                                out=KT_z[pb_ * 64:(pb_ + 1) * 64, :]
                                    .rearrange("p (j two t) -> p j two t", two=2, t=C)
                                    [:, :, pb_:pb_ + 1, :].squeeze(2),
                                in_=kt_ps[pb_ * 64:(pb_ + 1) * 64, :]
                                    .rearrange("p (j t) -> p j t", t=C)),
                                waits=[(ct, tkt)], inc=cv)
                      if attn_stage < 31:
                        continue
                      if True:
                        v_vw = r.op("vector", lambda e, c=c: e.tensor_tensor(
                            out=Vw_ch[:], in0=Vbf[:, c * D:(c + 1) * D], in1=b_full[:],
                            op=ALU.mult), waits=[(cs, ev_v_tc[c]), (cs, decay_s),
                                                 (ct, ct.n)], inc=cv)
                        if attn_stage < 32:
                            continue
                        for hh in range(2):
                            tsc = 0
                            for hi in range(8):
                                h = hh * 8 + hi
                                fc = h // 2
                                pb = (h % 2) * 64
                                tsc = r.op("tensor", lambda e, h=h, fc=fc, hi=hi, c=c:
                                           e.matmul(
                                    sc_ps[:, hi * 128:(hi + 1) * 128],
                                    KT_z[:, h * C:(h + 1) * C],
                                    QT[:, fc * SC + c * 128: fc * SC + (c + 1) * 128],
                                    start=True, stop=True),
                                    waits=([(cv, last_kt_v), (cv, last_A_v)] if hi == 0 else []),
                                    inc=(ct if hi == 7 else None), selfsync=False)
                            if attn_stage < 33:
                                continue
                            wA = [(ct, tsc), (cs, decay_s), (cv, decay_v)]
                            if not first_A_mm:
                                wA.append((ct, last_Y_t))
                            last_A_v = r.op("vector", lambda e, hh=hh: e.tensor_tensor(
                                out=A_bf[:, hh * 8 * C:(hh + 1) * 8 * C],
                                in0=sc_ps[:], in1=DT_all[:, hh * 8 * C:(hh + 1) * 8 * C],
                                op=ALU.mult), waits=wA, inc=cv)
                            first_A_mm = False
                            if attn_stage < 50:
                                continue
                            tY = 0
                            for hi in range(8):
                                h = hh * 8 + hi
                                fc = h // 2
                                pb = (h % 2) * 64
                                j = h // 2
                                tY = r.op("tensor", lambda e, h=h, c=c, pb=pb, j=j: e.matmul(
                                    y_ps[pb:pb + 64, j * 128:(j + 1) * 128],
                                    Vbf[:, c * D + h * DK: c * D + (h + 1) * DK],
                                    A_bf[:, h * C:(h + 1) * C], start=True, stop=False),
                                    waits=([(cv, last_A_v), (cs, last_yevac_s),
                                            (cv, last_state_v)] if hi == 0 else []),
                                    inc=None, selfsync=False)
                                tY = r.op("tensor", lambda e, h=h, c=c, pb=pb, j=j, fc=fc:
                                          e.matmul(
                                    y_ps[pb:pb + 64, j * 128:(j + 1) * 128],
                                    S_z[:, h * DK:(h + 1) * DK],
                                    Qa[:, fc * SC + c * 128: fc * SC + (c + 1) * 128],
                                    start=False, stop=True),
                                    inc=(ct if hi == 7 else None), selfsync=False)
                            last_Y_t = tY
                        if attn_stage < 50:
                            continue
                        last_yevac_s = r.op("scalar", lambda e, c=c: e.activation(
                            out=yT[:, c * 1024:(c + 1) * 1024],
                            in_=y_ps[:], func=AF.Copy),
                            waits=[(ct, last_Y_t)], inc=cs)
                        if attn_stage < 60:
                            continue
                        tst = 0
                        for h in range(H):
                            pb = (h % 2) * 64
                            j = h // 2
                            r.op("tensor", lambda e, h=h, c=c, pb=pb, j=j: e.matmul(
                                st_ps[pb:pb + 64, j * DK:(j + 1) * DK],
                                Kbf[:, c * D + h * DK:c * D + (h + 1) * DK],
                                Vw_ch[:, h * DK:(h + 1) * DK],
                                start=True, stop=False),
                                waits=([(cv, last_state_v), (cv, v_vw)] if h == 0 else []),
                                inc=None, selfsync=False)
                            tst = r.op("tensor", lambda e, h=h, pb=pb, j=j: e.matmul(
                                st_ps[pb:pb + 64, j * DK:(j + 1) * DK],
                                gI_z[:, h * DK:(h + 1) * DK],
                                S_z[:, h * DK:(h + 1) * DK],
                                start=False, stop=True),
                                inc=(ct if h == H - 1 else None), selfsync=False)
                        for pb_ in range(2):
                            last_state_v = r.op("vector", lambda e, pb_=pb_: e.tensor_copy(
                                out=S_z[pb_ * 64:(pb_ + 1) * 64, :]
                                    .rearrange("p (j two d) -> p j two d", two=2, d=DK)
                                    [:, :, pb_:pb_ + 1, :].squeeze(2),
                                in_=st_ps[pb_ * 64:(pb_ + 1) * 64, :]
                                    .rearrange("p (j d) -> p j d", d=DK)),
                                waits=([(ct, tst), (ct, last_Y_t)] if pb_ == 0 else []), inc=cv)
                        if isc + 1 < NSC:
                            emit_ln_tc(isc + 1, c)
                            if c == 3:
                                emit_ln_finish(isc + 1)
                        if attn_stage >= 70:
                            tc = c
                            pp = tc % 2
                            for n in range(2):
                                w0 = [(cs, last_yevac_s), WT] + ev_qkv[pp]
                                if tc == 0 and n == 0:
                                    w0.append((ss, st_s_prev))
                                for j in range(8):
                                    tw = r.op("tensor", lambda e, tc=tc, n=n, j=j, pp=pp:
                                              e.matmul(
                                        qkv_ps[pp][:],
                                        yT[:, tc * 1024 + j * 128: tc * 1024 + (j + 1) * 128],
                                        woT[:, j * D + n * 512: j * D + (n + 1) * 512],
                                        start=(j == 0), stop=(j == 7)),
                                        waits=(w0 if j == 0 else []),
                                        inc=(ct if j == 7 else None), selfsync=False)
                                ev_s = r.op("scalar", lambda e, tc=tc, n=n, pp=pp:
                                            e.activation(
                                    out=s_sb[:, tc * D + n * 512: tc * D + (n + 1) * 512],
                                    in_=qkv_ps[pp][:], func=AF.Copy),
                                    waits=[(ct, tw)], inc=cs)
                                ev_qkv[pp] = [(cs, ev_s)]

                    if attn_stage < 70:
                        continue
                    for tc in range(4):
                        st_s_prev = r.op("sync", lambda e, tc=tc, t0=t0: e.dma_start(
                            out=s_dram[t0 + tc * 128: t0 + (tc + 1) * 128, :],
                            in_=s_sb[:, tc * D:(tc + 1) * D]),
                            waits=[(cs, ev_s)], inc=ss, selfsync=False)
                    free_xnT[buf] = (ct, ct.n)

        phA = dict(t=ct.n, v=cv.n, s=cs.n, sxn=sxn.n, ss=ss.n)

        if do_ffn:
            bstack = contextlib.ExitStack()
            sbB, psB = mk_alloc(bstack)
            with bstack:
                NEE = NE + 4
                xnT_g = [sbB(f"xnT_g{i}", [128, 8 * GT], BF16) for i in range(2)]
                acc2 = [sbB(f"acc{i}", [128, 8 * D]) for i in range(2)]
                upT = [sbB(f"upT{i}", [128, 8 * 512], BF16) for i in range(2)]
                dnT = [sbB(f"dnT{i}", [128, 4 * D], BF16) for i in range(2)]
                hr = sbB("hr", [128, 2048], BF16)
                hp_sb = [sbB(f"hp_sb{i}", [128, 2048], BF16) for i in range(2)]
                gateT = sbB("gateT", [128, 8 * NE], BF16)
                ebias_b = sbB("ebias_b", [128, NE])
                ebrow = sbB("ebrow", [1, NE])
                comb = sbB("comb", [128, 8 * NE])
                rsc_all = sbB("rsc_all", [128, 8 * NE])
                rwork = sbB("rwork", [128, 8 * NE])
                rtmp = sbB("rtmp", [128, 8 * NE])
                rtm2 = sbB("rtm2", [128, 8 * NE])
                pr6 = sbB("pr6", [128, 6 * 64])
                gsco = sbB("gsco", [128, 3 * 64])
                gtmp = sbB("gtmp", [128, 64])
                mx8x = sbB("mx8x", [128, 64])
                mrep4x = sbB("mrep4x", [128, 64])
                mrep6x = sbB("mrep6x", [128, 64])
                rden = sbB("rden", [128, 8])
                gs8 = sbB("gs8", [128, 8 * D], BF16)
                fence_sb = sbB("fence_sb", [1, 8])
                s8 = sbB("s8", [128, 8 * D], BF16)


                h_ps = [psB(f"h_ps{i}", [128, 1024]) for i in range(2)]
                dn_ps = [psB(f"dn_ps{i}", [128, 1024]) for i in range(2)]

                cg = r_cg
                barrier = [(ct, phA["t"]), (cv, phA["v"]), (cs, phA["s"]),
                           (sxn, phA["sxn"]), (ss, phA["ss"]), (lwT, lwT.n)]

                for dc in range(8):
                    r.op("sync", lambda e, dc=dc: e.dma_start(
                        out=gateT[:, dc * NE:(dc + 1) * NE],
                        in_=gate_bf[:, dc * 128:(dc + 1) * 128], transpose=True),
                        waits=[(ldw1, ldw1.n)] + (barrier if dc == 0 else []),
                        inc=lgT, selfsync=False)
                r.op("sync", lambda e: e.dma_start(out=ebrow[:], in_=e_bias),
                     inc=lgT, selfsync=False)
                ld_gateT = ld_eb = (lgT, lgT.n)
                t_eb = r.op("tensor", lambda e: e.matmul(
                    dn_ps[0][:, 0:NE], ones_row[:], ebrow[:], start=True, stop=True),
                    waits=[ld_eb] + barrier, inc=ct, selfsync=False)
                v_eb = r.op("vector", lambda e: e.tensor_copy(
                    out=ebias_b[:], in_=dn_ps[0][:, 0:NE]),
                    waits=[(ct, t_eb)] + barrier, inc=cv)
                r.op("vector", lambda e: e.memset(mrep4x[:], 1e30), inc=cv)
                r.op("vector", lambda e: e.memset(mrep6x[:], 1e30), inc=cv)
                s_bar = r.op("scalar", lambda e: e.activation(
                    out=ebias_b[:, 0:1], in_=ebias_b[:, 0:1], func=AF.Copy),
                    waits=[(cv, v_eb)] + barrier, inc=cs)
                r.op("gpsimd", lambda e: e.memset(fence_sb[:], 0.0),
                     waits=barrier, inc=cg, selfsync=False)

                free_xnTg = [(ct, 0), (ct, 0)]
                free_w = [(ct, 0), (ct, 0)]

                ld_xnTg = [0, 0]
                ld_we = [0, 0]
                last_stt_pp = [(cv, v_eb), (cv, 0)]
                last_relu_s = [0, 0]
                last_hp_v = [0, 0]
                dn_done = [(ct, 0), (ct, 0)]
                hp_ready = [(cv, 0), (cv, 0)]
                router_prev = 0
                s_g_last = 0

                def load_group(g):
                    gb = g % 2
                    for dc in range(8):
                        r.op("sync", lambda e, dc=dc, g=g, gb=gb: e.dma_start(
                            out=xnT_g[gb][:, dc * GT:(dc + 1) * GT],
                            in_=xn_dram[g * GT:(g + 1) * GT, dc * 128:(dc + 1) * 128],
                            transpose=True),
                            waits=[(sxn, phA["sxn"]), free_xnTg[gb]],
                            inc=lxg[gb], selfsync=False)
                    ld_xnTg[gb] = (lxg[gb], lxg[gb].n)

                def load_expert(e_, slot):
                    wprep = (ldw1, ldw1.n)
                    if e_ < NE:
                        for dc in range(8):
                            r.op("sync", lambda e, dc=dc, e_=e_, slot=slot: e.dma_start(
                                out=upT[slot][:, dc * 512:(dc + 1) * 512],
                                in_=up_bf[e_][:, dc * 128:(dc + 1) * 128], transpose=True),
                                waits=[wprep, free_w[slot]], inc=lwe[slot], selfsync=False)
                        for ic in range(4):
                            r.op("sync", lambda e, ic=ic, e_=e_, slot=slot: e.dma_start(
                                out=dnT[slot][:, ic * D:(ic + 1) * D],
                                in_=down_bf[e_][:, ic * 128:(ic + 1) * 128], transpose=True),
                                waits=[wprep], inc=lwe[slot], selfsync=False)
                    else:
                        ep = e_ - NE
                        for dc in range(8):
                            r.op("sync", lambda e, dc=dc, ep=ep, slot=slot: e.dma_start(
                                out=upT[slot][:, dc * 512:(dc + 1) * 512],
                                in_=shup_bf[ep * 512:(ep + 1) * 512, dc * 128:(dc + 1) * 128],
                                transpose=True),
                                waits=[wprep, free_w[slot]], inc=lwe[slot], selfsync=False)
                        for ic in range(4):
                            r.op("sync", lambda e, ic=ic, ep=ep, slot=slot: e.dma_start(
                                out=dnT[slot][:, ic * D:(ic + 1) * D],
                                in_=shdown_bf[:, (ep * 4 + ic) * 128:(ep * 4 + ic + 1) * 128],
                                transpose=True),
                                waits=[wprep], inc=lwe[slot], selfsync=False)
                    ld_we[slot] = (lwe[slot], lwe[slot].n)

                load_group(0)
                if NGRP > 1:
                    load_group(1)
                load_expert(NE, 0)
                load_expert(NE + 1, 1)

                fin_done = [(cv, 0), (cv, 0)]
                pend_fin = None
                free_s8 = (cv, 0)
                free_gs8 = (cs, 0)
                def vfin_cur():
                    return max(last_stt_pp[0][1], last_stt_pp[1][1])

                def emit_finalize(gf):
                    nonlocal fin_done, free_gs8
                    g_, vfin = gf
                    acc = acc2[g_ % 2]
                    for i8 in range(8):
                        ls = r.op("gpsimd", lambda e, g_=g_, i8=i8: e.dma_start(
                            out=s8[:, i8 * D:(i8 + 1) * D],
                            in_=s_dram[g_ * GT + i8 * 128: g_ * GT + (i8 + 1) * 128, :]),
                            waits=([(ss, phA["ss"]), free_s8] if i8 == 0 else []),
                            inc=lsl[0], selfsync=False)
                    ls = (lsl[0], lsl[0].n)
                    s_all = r.op("scalar", lambda e, acc=acc: e.activation(
                        out=gs8[:], in_=acc[:], func=AF.Sigmoid),
                        waits=[(cv, vfin), free_gs8], inc=cs)
                    v_g8 = r.op("vector", lambda e, acc=acc: e.tensor_tensor(
                        out=gs8[:], in0=acc[:], in1=gs8[:], op=ALU.mult),
                        waits=[(cs, s_all)], inc=cv)
                    fin_done[g_ % 2] = (cv, v_g8)
                    v_sg = r.op("vector", lambda e: e.tensor_tensor(
                        out=gs8[:], in0=s8[:], in1=gs8[:], op=ALU.mult),
                        waits=[ls], inc=cv)
                    free_s8_v = (cv, v_sg)
                    for idx in range(8):
                        row0 = g_ * GT + idx * 128
                        b = row0 // L
                        l0 = row0 % L
                        r.op("gpsimd", lambda e, b=b, l0=l0, idx=idx: e.dma_start(
                            out=out[b, l0:l0 + 128, :],
                            in_=gs8[:, idx * D:(idx + 1) * D], accum_op=ALU.add),
                            waits=([(cv, v_sg), (pf, pf.n)] if idx == 0 else []),
                            inc=sto[0], selfsync=False)
                    free_gs8 = (sto[0], sto[0].n)
                    return free_s8_v


                for g in range(NGRP):
                    gb = g % 2
                    acc = acc2[g % 2]
                    v_z = 0
                    for i in range(8):
                        v_z = r.op("vector", lambda e, i=i, acc=acc: e.memset(
                            acc[:, i * D:(i + 1) * D], 0.0),
                            waits=([fin_done[g % 2]] if i == 0 else []), inc=cv)
                    s_sig2 = 0
                    for j2 in range(2):
                        tl = 0
                        for tl4 in range(4):
                            w0 = ([ld_xnTg[gb], ld_gateT,
                                   (cs, last_relu_s[j2]), (cg, last_hp_v[j2])]
                                  if tl4 == 0 else [])
                            for dc in range(8):
                                tl = r.op("tensor", lambda e, j2=j2, tl4=tl4, dc=dc, gb=gb:
                                          e.matmul(
                                    h_ps[j2][:, tl4 * 32:(tl4 + 1) * 32],
                                    xnT_g[gb][:, dc * GT + (j2 * 4 + tl4) * 128:
                                               dc * GT + (j2 * 4 + tl4 + 1) * 128],
                                    gateT[:, dc * NE:(dc + 1) * NE],
                                    start=(dc == 0), stop=(dc == 7)),
                                    waits=(w0 if dc == 0 else []),
                                    inc=(ct if (tl4 == 3 and dc == 7) else None),
                                    selfsync=False)
                        s_sig2 = r.op("scalar", lambda e, j2=j2: e.activation(
                            out=rsc_all[:, j2 * 128:(j2 + 1) * 128],
                            in_=h_ps[j2][:, 0:128], func=AF.Sigmoid),
                            waits=[(ct, tl), (cv, router_prev)], inc=cs)
                        last_relu_s[j2] = s_sig2

                    def vst(fn, waits=()):
                        return r.op("vector", fn, waits=waits, inc=cv, selfsync=False)
                    rw4 = rwork[:].rearrange("p (i g m) -> p i g m", g=8, m=4)
                    st = vst(lambda e: e.tensor_tensor(
                        out=rwork[:].rearrange("p (i e) -> p i e", e=NE),
                        in0=rsc_all[:].rearrange("p (i e) -> p i e", e=NE),
                        in1=ebias_b[:].unsqueeze(1).broadcast_to([128, 8, NE]),
                        op=ALU.add), waits=[(cs, s_sig2), (cv, v_eb), (cv, cv.n)])
                    pairs = [(0, 1), (2, 3), (0, 2), (1, 3), (0, 3), (1, 2)]
                    for pi, (pa, pb_) in enumerate(pairs):
                        st2 = vst(lambda e, pa=pa, pb_=pb_, pi=pi: e.tensor_tensor(
                            out=pr6[:, pi * 64:(pi + 1) * 64],
                            in0=rw4[:, :, :, pa:pa + 1].squeeze(3),
                            in1=rw4[:, :, :, pb_:pb_ + 1].squeeze(3), op=ALU.add),
                            waits=([(cv, st)] if pi == 0 else []))
                    st3 = None
                    for (o, a, b_) in ((0, 0, 1), (1, 2, 3), (2, 4, 5)):
                        st3 = vst(lambda e, o=o, a=a, b_=b_: e.tensor_tensor(
                            out=gsco[:, o * 64:(o + 1) * 64], in0=pr6[:, a * 64:(a + 1) * 64],
                            in1=pr6[:, b_ * 64:(b_ + 1) * 64], op=ALU.max),
                            waits=([(cv, st2)] if o == 0 else []))
                    st4 = vst(lambda e: e.tensor_tensor(
                        out=gsco[:, 0:64], in0=gsco[:, 0:64], in1=gsco[:, 64:128],
                        op=ALU.max), waits=[(cv, st3)])
                    st5 = vst(lambda e: e.tensor_tensor(
                        out=gsco[:, 0:64], in0=gsco[:, 0:64], in1=gsco[:, 128:192],
                        op=ALU.max), waits=[(cv, st4)])
                    st6 = None
                    for s_ in range(8):
                        st6 = vst(lambda e, s_=s_: e.max(
                            out=mx8x[:, s_ * 8:(s_ + 1) * 8], in_=gsco[:, s_ * 8:(s_ + 1) * 8]),
                            waits=([(cv, st5)] if s_ == 0 else []))
                    st7 = None
                    for s_ in range(8):
                        st7 = vst(lambda e, s_=s_: e.tensor_copy(
                            out=mrep4x[:, s_ * 8:s_ * 8 + 4], in_=mx8x[:, s_ * 8:s_ * 8 + 4]),
                            waits=([(cv, st6)] if s_ == 0 else []))
                    st8 = None
                    for s_ in range(8):
                        st8 = vst(lambda e, s_=s_: e.match_replace(
                            out=gtmp[:, s_ * 8:(s_ + 1) * 8], in_to_replace=mrep4x[:, s_ * 8:(s_ + 1) * 8],
                            in_values=gsco[:, s_ * 8:(s_ + 1) * 8], imm_value=-1e30),
                            waits=([(cv, st7)] if s_ == 0 else []))
                    st9 = vst(lambda e: e.tensor_scalar(
                        out=gtmp[:], in0=gtmp[:], scalar1=-1e29, scalar2=None,
                        op0=ALU.is_lt), waits=[(cv, st8)])
                    st10 = vst(lambda e: e.tensor_tensor(
                        out=rtmp[:].rearrange("p (i g m) -> p i g m", g=8, m=4),
                        in0=rw4,
                        in1=gtmp[:].rearrange("p (i g) -> p i g", g=8).unsqueeze(3)
                            .broadcast_to([128, 8, 8, 4]), op=ALU.mult),
                        waits=[(cv, st9)])
                    st11 = None
                    for s_ in range(8):
                        st11 = vst(lambda e, s_=s_: e.max(
                            out=mx8x[:, s_ * 8:(s_ + 1) * 8], in_=rtmp[:, s_ * 32:(s_ + 1) * 32]),
                            waits=([(cv, st10)] if s_ == 0 else []))
                    st12 = None
                    for s_ in range(8):
                        st12 = vst(lambda e, s_=s_: e.tensor_copy(
                            out=mrep6x[:, s_ * 8:s_ * 8 + 6], in_=mx8x[:, s_ * 8:s_ * 8 + 6]),
                            waits=([(cv, st11)] if s_ == 0 else []))
                    st13 = None
                    for s_ in range(8):
                        st13 = vst(lambda e, s_=s_: e.match_replace(
                            out=rtm2[:, s_ * 32:(s_ + 1) * 32], in_to_replace=mrep6x[:, s_ * 8:(s_ + 1) * 8],
                            in_values=rtmp[:, s_ * 32:(s_ + 1) * 32], imm_value=-1e30),
                            waits=([(cv, st12)] if s_ == 0 else []))
                    st14 = vst(lambda e: e.tensor_scalar(
                        out=rtm2[:], in0=rtm2[:], scalar1=-1e29, scalar2=None,
                        op0=ALU.is_lt), waits=[(cv, st13)])
                    st15 = vst(lambda e: e.tensor_tensor(
                        out=rwork[:], in0=rtm2[:], in1=rsc_all[:], op=ALU.mult),
                        waits=[(cv, st14)])
                    st16 = vst(lambda e: e.reduce_sum(
                        out=rden[:], in_=rwork[:].rearrange("p (i e) -> p i e", e=NE),
                        axis=AX.X), waits=[(cv, st15)])
                    st17 = vst(lambda e: e.tensor_scalar(
                        out=rden[:], in0=rden[:], scalar1=1e-20,
                        scalar2=1.0 / ROUTED_SCALE, op0=ALU.add, op1=ALU.mult),
                        waits=[(cv, st16)])
                    st18 = vst(lambda e: e.reciprocal(out=rden[:], in_=rden[:]),
                               waits=[(cv, st17)])
                    v_cmb_last = vst(lambda e: e.tensor_tensor(
                        out=comb[:].rearrange("p (i e) -> p i e", e=NE),
                        in0=rwork[:].rearrange("p (i e) -> p i e", e=NE),
                        in1=rden[:].unsqueeze(2).broadcast_to([128, 8, NE]),
                        op=ALU.mult), waits=[(cv, st18)])
                    router_prev = v_cmb_last
                    v_router = v_cmb_last

                    NU = GT // 512
                    eorder = list(range(NE, NEE)) + list(range(NE))
                    units = [(e_, u) for e_ in eorder for u in range(NU)]

                    def emit_up(k):
                        e_, u = units[k]
                        slot = (k // NU) % 2
                        kb = k % 2
                        for hp in range(2):
                            tu = 0
                            for ic2 in range(2):
                                ic = hp * 2 + ic2
                                w0 = ([ld_we[slot], ld_xnTg[gb],
                                       (cs, last_relu_s[hp])]
                                      if ic2 == 0 else [])
                                for dc in range(8):
                                    tu = r.op("tensor", lambda e, ic=ic, dc=dc, u=u, gb=gb,
                                              slot=slot, hp=hp, ic2=ic2: e.matmul(
                                        h_ps[hp][:, ic2 * 512:(ic2 + 1) * 512],
                                        upT[slot][:, dc * 512 + ic * 128:
                                                  dc * 512 + (ic + 1) * 128],
                                        xnT_g[gb][:, dc * GT + u * 512:
                                                  dc * GT + (u + 1) * 512],
                                        start=(dc == 0), stop=(dc == 7)),
                                        waits=(w0 if dc == 0 else []),
                                        inc=(ct if (ic2 == 1 and dc == 7) else None),
                                        selfsync=False)
                            last_relu_s[hp] = r.op("scalar", lambda e, hp=hp: e.activation(
                                out=hr[:, hp * 1024:(hp + 1) * 1024], in_=h_ps[hp][:],
                                func=AF.Relu),
                                waits=[(ct, tu), (cg, last_hp_v[hp])], inc=cs)
                            last_hp_v[hp] = r.op("gpsimd", lambda e, hp=hp, kb=kb:
                                                 e.tensor_tensor(
                                out=hp_sb[kb][:, hp * 1024:(hp + 1) * 1024],
                                in0=hr[:, hp * 1024:(hp + 1) * 1024],
                                in1=hr[:, hp * 1024:(hp + 1) * 1024], op=ALU.mult),
                                waits=[(cs, last_relu_s[hp]), dn_done[kb]],
                                inc=cg, selfsync=True)
                        hp_ready[kb] = (cg, last_hp_v[1])

                    def emit_dn(k):
                        e_, u = units[k]
                        slot = (k // NU) % 2
                        kb = k % 2
                        last_td = 0
                        for tcl in range(4):
                            pp = tcl % 2
                            td = 0
                            w0 = [hp_ready[kb], last_stt_pp[pp]]
                            for n in range(2):
                                for ic in range(4):
                                    td = r.op("tensor", lambda e, tcl=tcl, n=n, ic=ic,
                                              slot=slot, pp=pp, kb=kb: e.matmul(
                                        dn_ps[pp][:, n * 512:(n + 1) * 512],
                                        hp_sb[kb][:, ic * 512 + tcl * 128:
                                              ic * 512 + (tcl + 1) * 128],
                                        dnT[slot][:, ic * D + n * 512:
                                                 ic * D + (n + 1) * 512],
                                        start=(ic == 0), stop=(ic == 3)),
                                        waits=(w0 if (n == 0 and ic == 0) else []),
                                        inc=(ct if (n == 1 and ic == 3) else None),
                                        selfsync=False)
                            last_td = td
                            idx = u * 4 + tcl
                            acc_ = acc2[g % 2]
                            if e_ < NE:
                                vv = r.op("vector", lambda e, idx=idx, e_=e_, pp=pp, acc_=acc_:
                                          e.scalar_tensor_tensor(
                                    out=acc_[:, idx * D:(idx + 1) * D], in0=dn_ps[pp][:],
                                    scalar=comb[:, idx * NE + e_: idx * NE + e_ + 1],
                                    in1=acc_[:, idx * D:(idx + 1) * D],
                                    op0=ALU.mult, op1=ALU.add),
                                    waits=[(ct, td), (cv, v_router), (cv, v_z)], inc=cv)
                            else:
                                vv = r.op("vector", lambda e, idx=idx, pp=pp, acc_=acc_:
                                          e.tensor_tensor(
                                    out=acc_[:, idx * D:(idx + 1) * D], in0=dn_ps[pp][:],
                                    in1=acc_[:, idx * D:(idx + 1) * D], op=ALU.add),
                                    waits=[(ct, td), (cv, v_z)], inc=cv)
                            last_stt_pp[pp] = (cv, vv)
                        dn_done[kb] = (ct, last_td)
                        if u == NU - 1:
                            free_w[slot] = (ct, last_td)
                            pos = k // NU
                            nxt = pos + 2
                            if nxt < NEE:
                                load_expert(eorder[nxt], nxt % 2)
                            elif g + 1 < NGRP:
                                load_expert(eorder[nxt - NEE], (nxt - NEE) % 2)

                    emit_up(0)
                    for k in range(len(units)):
                        if k + 1 < len(units):
                            emit_up(k + 1)
                        emit_dn(k)
                        if k == 3 and pend_fin is not None:
                            free_s8 = emit_finalize(pend_fin)
                            pend_fin = None
                    free_xnTg[gb] = (ct, ct.n)
                    if g + 2 < NGRP:
                        load_group(g + 2)
                    pend_fin = (g, vfin_cur())

                if pend_fin is not None:
                    free_s8 = emit_finalize(pend_fin)
                    pend_fin = None

        final_waits = [(sto[0], sto[0].n), (sto[1], sto[1].n)]
        for c_ in [ldw0, ldw0b, ldw1, lcst, lwT, lxA[0], lxA[1], lxnT[0], lxnT[1],
                   sxn, ss, lgT, lxg[0], lxg[1], lwe[0], lwe[1], lxl[0], lxl[1],
                   lsl[0], lsl[1], lcw[0], lcw[1], pf]:
            if c_.n > 0:
                final_waits.append((c_, c_.n))
        r.op("sync", lambda e: None, waits=final_waits, selfsync=False)

        block = ostack.enter_context(nc.Block())
        r.flush(block)

    return nc


def split_inputs(inp, n_cores=8):
    xs = np.ascontiguousarray(inp["L0_attn_x"])
    nb = xs.shape[0] // n_cores
    hc = host_consts()
    maps = []
    for i in range(n_cores):
        m = dict(
            x=np.ascontiguousarray(xs[i * nb:(i + 1) * nb]),
            w_qkv=inp["L0_attn_W_qkv"],
            w_o=inp["L0_attn_W_o"],
            log_gamma=np.asarray(inp["L0_attn_log_gamma"]).reshape(1, H),
            gate_w=inp["L0_ffn_gate_w"],
            e_bias=np.asarray(inp["L0_ffn_e_bias"]).reshape(1, NE),
            up_w=inp["L0_ffn_up_w"],
            down_w=inp["L0_ffn_down_w"],
            shup_w=inp["L0_ffn_shared_up_w"],
            shdown_w=inp["L0_ffn_shared_down_w"],
        )
        m.update(hc)
        maps.append({k: np.ascontiguousarray(np.asarray(v)) for k, v in m.items()})
    return maps


_NC_CACHE = {}


def kernel(**inputs):
    n_cores = 8
    xs = np.ascontiguousarray(np.asarray(inputs["L0_attn_x"], dtype=np.float32))
    nb = xs.shape[0] // n_cores
    if "nc" not in _NC_CACHE:
        _NC_CACHE["nc"] = build(n_batches=nb)
    nc = _NC_CACHE["nc"]
    maps = split_inputs(inputs, n_cores=n_cores)
    from concourse.bass_utils import run_bass_kernel_spmd
    res = run_bass_kernel_spmd(nc, maps, core_ids=list(range(n_cores)))
    out = np.concatenate([np.asarray(r["out"]) for r in res.results], axis=0)
    return out


# version 16
# speedup vs baseline: 1.0977x; 1.0977x over previous
import contextlib
import numpy as np
import concourse.bass as bass
import concourse.mybir as mybir

F32 = mybir.dt.float32
BF16 = mybir.dt.bfloat16
AF = mybir.ActivationFunctionType
ALU = mybir.AluOpType
AX = mybir.AxisListType

D = 1024
H = 16
DK = 64
C = 128
SC = 512
NE = 32
EI = 512
SI = 2048
LN_EPS = 1e-5
GT = 1024
ROUTED_SCALE = 2.5
L = 2048

ENGINES = ("sync", "tensor", "vector", "scalar", "gpsimd")


def host_consts():
    import ml_dtypes
    t = np.arange(C, dtype=np.float32)
    deltaT = np.maximum(t[None, :] - t[:, None], 0.0).astype(np.float32)
    causalT = (t[None, :] >= t[:, None]).astype(np.float32)
    tmod1 = np.broadcast_to(t + 1.0, (C, C)).astype(np.float32).copy()
    revcol = (C - 1 - t).reshape(C, 1).astype(np.float32)
    ident_bf = np.eye(C).astype(ml_dtypes.bfloat16)
    return dict(c_deltaT=deltaT, c_causalT=causalT, c_tmod1=tmod1,
                c_revcol=revcol, c_ident=ident_bf)


class Ctr:
    def __init__(self, sem, step=1):
        self.sem = sem
        self.n = 0
        self.step = step


class Rec:
    def __init__(self):
        self.ops = {e: [] for e in ENGINES}

    def op(self, eng, fn, waits=(), inc=None, selfsync=None):
        wl = []
        for (c, v) in waits:
            assert v <= c.n, f"wait on future value {v} > {c.n}"
            if v > 0:
                wl.append((c.sem, v))
        if selfsync is None:
            selfsync = eng in ("vector", "scalar")
        post = None
        if inc is not None:
            if selfsync and inc.n > 0:
                wl.append((inc.sem, inc.n))
            inc.n += inc.step
            post = inc.n
        self.ops[eng].append((wl, fn, (inc.sem, inc.step) if inc else None))
        return post

    def flush(self, block):
        for eng in ENGINES:
            entries = self.ops[eng]
            if not entries:
                continue

            def make(entries):
                def run(e):
                    for (wl, fn, inc) in entries:
                        for (sem, v) in wl:
                            e.wait_ge(sem, v)
                        inst = fn(e)
                        if inc is not None:
                            assert inst is not None
                            inst.then_inc(inc[0], inc[1])
                return run
            getattr(block, eng)(make(entries))


def build(n_batches=4, debug=False, do_attn=True, do_ffn=True,
          n_sc_limit=None, n_grp_limit=None, attn_stage=99):
    T_core = n_batches * L
    NSC = (T_core // SC) if n_sc_limit is None else n_sc_limit
    NGRP = (T_core // GT) if n_grp_limit is None else n_grp_limit
    nc = bass.Bass("TRN2", target_bir_lowering=False)

    x_in = nc.dram_tensor("x", [n_batches, L, D], F32, kind="ExternalInput").ap()
    w_qkv = nc.dram_tensor("w_qkv", [3 * D, D], F32, kind="ExternalInput").ap()
    w_o = nc.dram_tensor("w_o", [D, D], F32, kind="ExternalInput").ap()
    log_gamma = nc.dram_tensor("log_gamma", [1, H], F32, kind="ExternalInput").ap()
    gate_w = nc.dram_tensor("gate_w", [NE, D], F32, kind="ExternalInput").ap()
    e_bias = nc.dram_tensor("e_bias", [1, NE], F32, kind="ExternalInput").ap()
    up_w = nc.dram_tensor("up_w", [NE, EI, D], F32, kind="ExternalInput").ap()
    down_w = nc.dram_tensor("down_w", [NE, D, EI], F32, kind="ExternalInput").ap()
    shup_w = nc.dram_tensor("shup_w", [SI, D], F32, kind="ExternalInput").ap()
    shdown_w = nc.dram_tensor("shdown_w", [D, SI], F32, kind="ExternalInput").ap()
    hc = host_consts()
    cst = {}
    for n, a in hc.items():
        dt = BF16 if a.dtype.name == "bfloat16" else F32
        cst[n] = nc.dram_tensor(n, list(a.shape), dt, kind="ExternalInput").ap()

    out = nc.dram_tensor("out", [n_batches, L, D], F32, kind="ExternalOutput").ap()

    wqkv_bf = nc.dram_tensor("wqkv_bf", [3 * D, D], BF16, kind="Internal").ap()
    wo_bf = nc.dram_tensor("wo_bf", [D, D], BF16, kind="Internal").ap()
    gate_bf = nc.dram_tensor("gate_bf", [NE, D], BF16, kind="Internal").ap()
    up_bf = nc.dram_tensor("up_bf", [NE, EI, D], BF16, kind="Internal").ap()
    down_bf = nc.dram_tensor("down_bf", [NE, D, EI], BF16, kind="Internal").ap()
    shup_bf = nc.dram_tensor("shup_bf", [SI, D], BF16, kind="Internal").ap()
    shdown_bf = nc.dram_tensor("shdown_bf", [D, SI], BF16, kind="Internal").ap()
    dkind = "ExternalOutput" if debug else "Internal"
    xn_dram = nc.dram_tensor("xn_scratch", [T_core, D], BF16, kind=dkind).ap()
    s_dram = nc.dram_tensor("s_scratch", [T_core, D], BF16, kind=dkind).ap()

    ostack = contextlib.ExitStack()
    with ostack:
        def mk_alloc(stk):
            def sb(name, shape, dt=F32):
                return stk.enter_context(nc.sbuf_tensor(name, shape, dt))
            def psb(name, shape, dt=F32):
                return stk.enter_context(nc.psum_tensor(name, shape, dt))
            return sb, psb
        sb, psb = mk_alloc(ostack)
        mksem = lambda name: ostack.enter_context(nc.semaphore(name))

        r = Rec()
        ct = Ctr(mksem("ct"))
        cv = Ctr(mksem("cv"))
        cs = Ctr(mksem("cs"))
        r_cg = Ctr(mksem("cg"))
        ldw0 = Ctr(mksem("ldw0"), 16)
        ldw0b = Ctr(mksem("ldw0b"), 16)
        ldw1 = Ctr(mksem("ldw1"), 16)
        lcst = Ctr(mksem("lcst"), 16)
        lwT = Ctr(mksem("lwT"), 16)
        lxA = [Ctr(mksem("lxA0"), 16), Ctr(mksem("lxA1"), 16)]
        lxnT = [Ctr(mksem("lxnT0"), 16), Ctr(mksem("lxnT1"), 16)]
        sxn = Ctr(mksem("sxn"), 16)
        ss = Ctr(mksem("ss"), 16)
        lgT = Ctr(mksem("lgT"), 16)
        lxg = [Ctr(mksem("lxg0"), 16), Ctr(mksem("lxg1"), 16)]
        lwe = [Ctr(mksem("lwe0"), 16), Ctr(mksem("lwe1"), 16)]
        lxl = [Ctr(mksem("lxl0"), 16), Ctr(mksem("lxl1"), 16)]
        lsl = [Ctr(mksem("lsl0"), 16), Ctr(mksem("lsl1"), 16)]
        sto = [Ctr(mksem("sto0"), 16), Ctr(mksem("sto1"), 16)]

        ones_row = sb("ones_row", [1, 128])
        lgb = sb("lgb", [128, H])
        cstage = [sb(f"cstage{i}", [128, 2048]) for i in range(2)]
        cbf = [sb(f"cbf{i}", [128, 2048], BF16) for i in range(2)]

        cast_jobs = []

        def cast_dma(dst, srcw, ctr):
            cast_jobs.append((dst, srcw, ctr))

        cast_dma(wqkv_bf[:, 0:512], w_qkv[:, 0:512], ldw0)
        cast_dma(wqkv_bf[:, 512:1024], w_qkv[:, 512:1024], ldw0b)
        cast_dma(wo_bf, w_o, ldw0)
        cast_dma(gate_bf, gate_w, ldw1)
        cast_dma(shup_bf, shup_w, ldw1)
        cast_dma(shdown_bf, shdown_w, ldw1)
        for e_ in range(NE):
            cast_dma(up_bf[e_], up_w[e_], ldw1)
            cast_dma(down_bf[e_], down_w[e_], ldw1)

        pf = Ctr(mksem("pf"), 16)

        astack = contextlib.ExitStack()
        sbA, psA = mk_alloc(astack)
        ident_sb = sbA("ident_sb", [128, 128], BF16)
        DT_all = sbA("DT_all", [128, H * C])
        a_sm = sbA("a_sm", [128, 8 * C], BF16)
        b_full = sbA("b_full", [128, D])
        gI_z = sbA("gI_z", [128, H * DK], BF16)
        pstack = contextlib.ExitStack()
        sbP, _psP = mk_alloc(pstack)
        deltaT_sb = sbP("deltaT_sb", [128, 128])
        causalT_sb = sbP("causalT_sb", [128, 128])
        tmod1_sb = sbP("tmod1_sb", [128, 128])
        revcol_sb = sbP("revcol_sb", [128, 1])
        lgrow = sbP("lgrow", [1, H])
        lgtmp = sbP("lgtmp", [1, H])
        atmp = sbP("atmp", [128, 8 * C])
        btmp = sbP("btmp", [128, D])
        gcs = sbP("gcs", [128, H])
        lgQ = sbP("lgQ", [128, 8])

        dstm = dict(c_deltaT=deltaT_sb, c_causalT=causalT_sb, c_tmod1=tmod1_sb,
                    c_revcol=revcol_sb, c_ident=ident_sb)
        for n in dstm:
            r.op("sync", lambda e, d=dstm[n], s=cst[n]: e.dma_start(out=d[:], in_=s),
                 inc=lcst, selfsync=False)
        r.op("sync", lambda e: e.dma_start(out=lgrow[:], in_=log_gamma),
             inc=lcst, selfsync=False)
        CST = lambda: (lcst, lcst.n)

        lcw = [Ctr(mksem("lcw0"), 16), Ctr(mksem("lcw1"), 16)]
        ccg = Ctr(mksem("ccg"))

        def emit_casts():
            chunks = []
            for (dstw, srcw, ctr) in cast_jobs:
                R, W = srcw.shape
                rows = (2048 * 128 // W) if W <= 2048 else 0
                assert rows > 0 and 2048 % W == 0 and dstw.shape == (R, W)
                r0 = 0
                while r0 < R:
                    nr = min(rows, R - r0)
                    if nr % 128 == 0:
                        sv = srcw[r0:r0 + nr, :].rearrange("(c p) w -> p c w", p=128)
                        dv = dstw[r0:r0 + nr, :].rearrange("(c p) w -> p c w", p=128)
                        pshape, fw = 128, (nr // 128) * W
                    else:
                        sv = srcw[r0:r0 + nr, :]
                        dv = dstw[r0:r0 + nr, :]
                        pshape, fw = nr, W
                    chunks.append((sv, dv, pshape, fw, W, ctr))
                    r0 += nr

            prev_copy = [0, 0]
            st_free = [None, None]
            ldval = {}

            def dma_view(buf, sl, pshape, fw, W):
                v = buf[sl][0:pshape, 0:fw]
                if pshape == 128 and fw != W:
                    v = v.rearrange("p (c w) -> p c w", w=W)
                return v

            def emit_load(k):
                sv, dv, pshape, fw, W, ctr = chunks[k]
                sl = k % 2
                stg = dma_view(cstage, sl, pshape, fw, W)
                w = [(ccg, prev_copy[sl])] if prev_copy[sl] else []
                r.op("gpsimd", lambda e, sv=sv, stg=stg: e.dma_start(
                    out=stg, in_=sv), waits=w, inc=lcw[sl], selfsync=False)
                ldval[k] = lcw[sl].n

            emit_load(0)
            if len(chunks) > 1:
                emit_load(1)
            for k in range(len(chunks)):
                sv, dv, pshape, fw, W, ctr = chunks[k]
                sl = k % 2
                wl = [(lcw[sl], ldval[k])]
                if st_free[sl] is not None:
                    wl.append(st_free[sl])
                c = r.op("gpsimd", lambda e, sl=sl, pshape=pshape, fw=fw:
                         e.tensor_copy(out=cbf[sl][0:pshape, 0:fw],
                                       in_=cstage[sl][0:pshape, 0:fw]),
                         waits=wl, inc=ccg, selfsync=False)
                prev_copy[sl] = c
                sbf = dma_view(cbf, sl, pshape, fw, W)
                r.op("gpsimd", lambda e, dv=dv, sbf=sbf: e.dma_start(
                    out=dv, in_=sbf), waits=[(ccg, c)], inc=ctr, selfsync=False)
                st_free[sl] = (ctr, ctr.n)
                if k + 2 < len(chunks):
                    emit_load(k + 2)

        v_ones = r.op("vector", lambda e: e.memset(ones_row[:], 1.0), inc=cv)
        s1 = r.op("scalar", lambda e: e.activation(out=lgtmp[:], in_=lgrow[:], func=AF.Exp,
                                                   scale=-1.0), waits=[CST()], inc=cs)
        v1 = r.op("vector", lambda e: e.tensor_scalar(out=lgtmp[:], in0=lgtmp[:], scalar1=1e30,
                                                      scalar2=None, op0=ALU.min),
                  waits=[(cs, s1)], inc=cv)
        s2 = r.op("scalar", lambda e: e.activation(out=lgtmp[:], in_=lgtmp[:], func=AF.Ln,
                                                   bias=1.0), waits=[(cv, v1)], inc=cs)
        v2 = r.op("vector", lambda e: e.tensor_scalar(out=lgrow[:], in0=lgtmp[:], scalar1=-1.0,
                                                      scalar2=None, op0=ALU.mult),
                  waits=[(cs, s2)], inc=cv)
        with nc.psum_tensor("lg_ps", [128, 512], F32) as lg_ps_t:
            lg_ps = lg_ps_t.ap()
            t1 = r.op("tensor", lambda e: e.matmul(lg_ps[:, 0:H], ones_row[:], lgrow[:],
                                                   start=True, stop=True),
                      waits=[(cv, v2), (cv, v_ones)], inc=ct)
            v_lgb = r.op("vector", lambda e: e.tensor_copy(out=lgb[:], in_=lg_ps[:, 0:H]),
                         waits=[(ct, t1)], inc=cv)
        v4 = r.op("vector", lambda e: e.tensor_scalar(
            out=btmp[:].rearrange("p (h d) -> p h d", d=DK),
            in0=lgb[:].unsqueeze(2).broadcast_to([128, H, DK]),
            scalar1=revcol_sb[:, 0:1], scalar2=None, op0=ALU.mult),
            waits=[(cv, v_lgb), CST()], inc=cv)
        r.op("scalar", lambda e: e.activation(out=b_full[:], in_=btmp[:], func=AF.Exp),
             waits=[(cv, v4)], inc=cs)
        v5 = r.op("vector", lambda e: e.tensor_scalar(out=gcs[:], in0=lgb[:], scalar1=float(C),
                                                      scalar2=None, op0=ALU.mult),
                  waits=[(cv, v_lgb)], inc=cv)
        sgc = r.op("scalar", lambda e: e.activation(out=gcs[:], in_=gcs[:], func=AF.Exp),
                   waits=[(cv, v5)], inc=cs)
        r.op("vector", lambda e: e.memset(gI_z[:], 0.0), inc=cv)
        for pb in range(2):
            r.op("vector", lambda e, pb=pb: e.tensor_tensor(
                out=gI_z[pb * 64:(pb + 1) * 64, :]
                    .rearrange("p (j two d) -> p j two d", two=2, d=DK)
                    [:, :, pb:pb + 1, :].squeeze(2),
                in0=ident_sb[pb * 64:(pb + 1) * 64, pb * 64:(pb + 1) * 64]
                    .unsqueeze(1).broadcast_to([64, 8, DK]),
                in1=gcs[pb * 64:(pb + 1) * 64, pb:H:2].unsqueeze(2).broadcast_to([64, 8, DK]),
                op=ALU.mult),
                waits=[(cs, sgc), CST()], inc=cv)
            r.op("vector", lambda e, pb=pb: e.tensor_copy(
                out=lgQ[pb * 64:(pb + 1) * 64, :],
                in_=lgb[pb * 64:(pb + 1) * 64, pb:H:2]),
                waits=[(cv, v_lgb)], inc=cv)
        v_lgq = cv.n
        vd1 = r.op("vector", lambda e: e.tensor_tensor(
            out=DT_all[:].rearrange("p (h t) -> p h t", t=C),
            in0=deltaT_sb[:].unsqueeze(1).broadcast_to([128, H, C]),
            in1=lgb[:].unsqueeze(2).broadcast_to([128, H, C]), op=ALU.mult),
            waits=[(cv, v_lgb), CST()], inc=cv)
        sd1 = r.op("scalar", lambda e: e.activation(out=DT_all[:], in_=DT_all[:], func=AF.Exp),
                   waits=[(cv, vd1)], inc=cs)
        r.op("vector", lambda e: e.tensor_tensor(
            out=DT_all[:].rearrange("p (h t) -> p h t", t=C),
            in0=DT_all[:].rearrange("p (h t) -> p h t", t=C),
            in1=causalT_sb[:].unsqueeze(1).broadcast_to([128, H, C]), op=ALU.mult),
            waits=[(cs, sd1), CST()], inc=cv)
        va = r.op("vector", lambda e: e.tensor_tensor(
            out=atmp[:].rearrange("p (f t) -> p f t", t=C),
            in0=tmod1_sb[:].unsqueeze(1).broadcast_to([128, 8, C]),
            in1=lgQ[:].unsqueeze(2).broadcast_to([128, 8, C]), op=ALU.mult),
            waits=[(cv, v_lgq), CST()], inc=cv)
        r.op("scalar", lambda e: e.activation(out=a_sm[:], in_=atmp[:], func=AF.Exp),
             waits=[(cv, va)], inc=cs)
        decay_s = cs.n
        decay_v = cv.n
        emit_casts()
        for b_ in range(n_batches):
            r.op("gpsimd", lambda e, b_=b_: e.dma_start(out=out[b_], in_=x_in[b_]),
                 inc=pf, selfsync=False)
        pstack.close()

        with astack:
            wqkvT = sbA("wqkvT", [128, 8 * 3 * D], BF16)
            woT = sbA("woT", [128, 8 * D], BF16)
            if do_attn:
                x_tc = [sbA(f"x_tc{i}", [128, D]) for i in range(2)]
                xn_sc = sbA("xn_sc", [128, 4 * D], BF16)
                xnT_sc = [sbA(f"xnT_sc{i}", [128, 8 * SC], BF16) for i in range(2)]
                s_sb = sbA("s_sb", [128, 4 * D], BF16)
                stat = sbA("stat", [128, 8])
                sqtmp = sbA("sqtmp", [128, D])
                kstat = sbA("kstat", [128, 64])
                QT = sbA("QT", [128, 8 * SC], BF16)
                Qa = sbA("Qa", [128, 8 * SC], BF16)
                Kf = sbA("Kf", [128, D])
                Kbf = sbA("Kbf", [128, 4 * D], BF16)
                Vbf = sbA("Vbf", [128, 4 * D], BF16)
                Vw_ch = sbA("Vw_ch", [128, D], BF16)
                KT_z = sbA("KT_z", [128, H * C], BF16)
                A_bf = sbA("A_bf", [128, H * C], BF16)
                yT = sbA("yT", [128, 8 * SC], BF16)
                S_z = sbA("S_z", [128, H * DK], BF16)

                qkv_ps = [psA(f"qkv_ps{i}", [128, 512]) for i in range(2)]
                kt_ps = psA("kt_ps", [128, 1024], BF16)
                sc_ps = psA("sc_ps", [128, 8 * C])
                y_ps = psA("y_ps", [128, 8 * C])
                st_ps = psA("st_ps", [128, 8 * DK])

                free_x = [(cv, 0), (cv, 0)]
                free_xnT = [(ct, 0), (ct, 0)]
                ev_qkv = [[], []]
                st_xn_prev = 0
                st_s_prev = 0
                last_state_v = 0
                last_yevac_s = 0
                last_kt_v = 0
                last_A_v = 0
                last_Y_t = 0
                last_ksq_v = 0
                last_kln_v = 0
                v_xn_prev = 0
                ev_s = 0
                vxn_tc = [0, 0, 0, 0]
                first_A_mm = True

                r.op("vector", lambda e: e.memset(KT_z[:], 0.0), inc=cv)
                lnout = {}

                def emit_ln_tc(isc, tc):
                    nonlocal st_xn_prev, last_ksq_v, v_xn_prev
                    b, sc = divmod(isc, L // SC)
                    if True:
                        xb = (isc * 4 + tc) % 2
                        xa = x_in[b, sc * SC + tc * 128: sc * SC + (tc + 1) * 128, :]
                        r.op("sync", lambda e, xa=xa, xb=xb: e.dma_start(
                            out=x_tc[xb][:], in_=xa), waits=[free_x[xb]], inc=lxA[xb],
                            selfsync=False)
                        ld_x = (lxA[xb], lxA[xb].n)
                        s_sq = r.op("scalar", lambda e, xb=xb: e.activation(
                            out=sqtmp[:], in_=x_tc[xb][:], func=AF.Square,
                            accum_out=stat[:, 1:2]),
                            waits=[ld_x, (cv, last_ksq_v), (cv, v_xn_prev)], inc=cs)
                        s_sum = r.op("scalar", lambda e, xb=xb: e.activation(
                            out=sqtmp[:], in_=x_tc[xb][:], func=AF.Identity,
                            accum_out=stat[:, 0:1]),
                            waits=[], inc=cs)
                        r.op("scalar", lambda e: e.activation(
                            out=stat[:, 2:3], in_=stat[:, 0:1], func=AF.Identity,
                            scale=1.0 / D), inc=cs)
                        r.op("scalar", lambda e: e.activation(
                            out=stat[:, 3:4], in_=stat[:, 2:3], func=AF.Square), inc=cs)
                        s_va = r.op("scalar", lambda e: e.activation(
                            out=stat[:, 1:2], in_=stat[:, 1:2], func=AF.Identity,
                            scale=1.0 / D), inc=cs)
                        v_v2 = r.op("vector", lambda e: e.tensor_scalar(
                            out=stat[:, 1:2], in0=stat[:, 1:2], scalar1=stat[:, 3:4],
                            scalar2=LN_EPS, op0=ALU.subtract, op1=ALU.add),
                            waits=[(cs, s_va)], inc=cv)
                        s_std = r.op("scalar", lambda e: e.activation(
                            out=stat[:, 4:5], in_=stat[:, 1:2], func=AF.Sqrt),
                            waits=[(cv, v_v2)], inc=cs)
                        r.op("vector", lambda e: e.reciprocal(
                            out=stat[:, 5:6], in_=stat[:, 4:5]), waits=[(cs, s_std)], inc=cv)
                        wv = [(sxn, st_xn_prev)] if tc == 0 else []
                        v_xn = r.op("vector", lambda e, tc=tc, xb=xb: e.tensor_scalar(
                            out=xn_sc[:, tc * D:(tc + 1) * D], in0=x_tc[xb][:],
                            scalar1=stat[:, 2:3], scalar2=stat[:, 5:6],
                            op0=ALU.subtract, op1=ALU.mult), waits=wv, inc=cv)
                        free_x[xb] = (cv, v_xn)
                        v_xn_prev = v_xn
                        vxn_tc[tc] = v_xn

                def emit_ln_finish(isc):
                    nonlocal st_xn_prev
                    t0 = isc * SC
                    buf = isc % 2
                    for tc in range(4):
                        st_xn_prev = r.op("sync", lambda e, tc=tc, t0=t0: e.dma_start(
                            out=xn_dram[t0 + tc * 128: t0 + (tc + 1) * 128, :],
                            in_=xn_sc[:, tc * D:(tc + 1) * D]),
                            waits=[(cv, vxn_tc[tc])], inc=sxn, selfsync=False)
                    st_xn = st_xn_prev
                    for dc in range(8):
                        r.op("sync", lambda e, dc=dc, buf=buf, t0=t0: e.dma_start(
                            out=xnT_sc[buf][:, dc * SC:(dc + 1) * SC],
                            in_=xn_dram[t0:t0 + SC, dc * 128:(dc + 1) * 128], transpose=True),
                            waits=[(sxn, st_xn), free_xnT[buf]], inc=lxnT[buf], selfsync=False)
                    lnout[isc] = (lxnT[buf], lxnT[buf].n)

                def emit_ln(isc):
                    for tc in range(4):
                        emit_ln_tc(isc, tc)
                    emit_ln_finish(isc)

                emit_ln(0)
                ld_wqkvT_list = []
                for dc in range(8):
                    wc = (ldw0, ldw0.n) if dc < 4 else (ldw0b, ldw0b.n)
                    r.op("sync", lambda e, dc=dc: e.dma_start(
                        out=wqkvT[:, dc * 3 * D:(dc + 1) * 3 * D],
                        in_=wqkv_bf[:, dc * 128:(dc + 1) * 128], transpose=True),
                        waits=[wc] +
                              ([(cv, decay_v), (cs, decay_s)] if dc == 0 else []),
                        inc=lwT, selfsync=False)
                for dc in range(8):
                    r.op("sync", lambda e, dc=dc: e.dma_start(
                        out=woT[:, dc * D:(dc + 1) * D],
                        in_=wo_bf[:, dc * 128:(dc + 1) * 128], transpose=True),
                        waits=[(ldw0, ldw0.n)], inc=lwT, selfsync=False)
                WT = (lwT, lwT.n)

                for isc in range(NSC):
                    b, sc = divmod(isc, L // SC)
                    t0 = isc * SC
                    buf = isc % 2
                    if sc == 0:
                        last_state_v = r.op("vector", lambda e: e.memset(S_z[:], 0.0),
                                            waits=[(ct, last_Y_t)], inc=cv)
                    ld_xnT = lnout[isc]

                    if attn_stage < 2:
                        continue
                    ev_q = [0] * 8
                    for fc in range(8):
                        pp = fc % 2
                        w0 = [ld_xnT, WT, (cv, v_lgb)] + ev_qkv[pp]
                        for dc in range(8):
                            tq = r.op("tensor", lambda e, fc=fc, dc=dc, buf=buf, pp=pp: e.matmul(
                                qkv_ps[pp][:],
                                wqkvT[:, dc * 3 * D + fc * 128: dc * 3 * D + (fc + 1) * 128],
                                xnT_sc[buf][:, dc * SC:(dc + 1) * SC],
                                start=(dc == 0), stop=(dc == 7)),
                                waits=(w0 if dc == 0 else []),
                                inc=(ct if dc == 7 else None), selfsync=False)
                        ev_q[fc] = r.op("scalar", lambda e, fc=fc, pp=pp: e.activation(
                            out=QT[:, fc * SC:(fc + 1) * SC], in_=qkv_ps[pp][:], func=AF.Copy),
                            waits=[(ct, tq)], inc=cs)
                        ev_qkv[pp] = [(cs, ev_q[fc])]
                    v_qa = r.op("vector", lambda e: e.tensor_tensor(
                        out=Qa[:].rearrange("p (f c t) -> p f c t", c=4, t=C),
                        in0=QT[:].rearrange("p (f c t) -> p f c t", c=4, t=C),
                        in1=a_sm[:].rearrange("p (f t) -> p f t", t=C).unsqueeze(2)
                            .broadcast_to([128, 8, 4, C]), op=ALU.mult),
                        waits=[(cs, ev_q[7]), (cs, decay_s)], inc=cv)
                    ev_v_tc = [0] * 4
                    v_kln_tc = [0] * 4
                    for tc in range(4):
                        for nch in range(4):
                            pp = nch % 2
                            w0 = list(ev_qkv[pp])
                            for dc in range(8):
                                tkv = r.op("tensor", lambda e, tc=tc, nch=nch, dc=dc, buf=buf,
                                           pp=pp: e.matmul(
                                    qkv_ps[pp][:],
                                    xnT_sc[buf][:, dc * SC + tc * 128: dc * SC + (tc + 1) * 128],
                                    wqkvT[:, dc * 3 * D + D + nch * 512:
                                          dc * 3 * D + D + (nch + 1) * 512],
                                    start=(dc == 0), stop=(dc == 7)),
                                    waits=(w0 if dc == 0 else []),
                                    inc=(ct if dc == 7 else None), selfsync=False)
                            if nch < 2:
                                ev = r.op("scalar", lambda e, nch=nch, pp=pp: e.activation(
                                    out=Kf[:, nch * 512:(nch + 1) * 512], in_=qkv_ps[pp][:],
                                    func=AF.Copy),
                                    waits=[(ct, tkv), (cv, last_kln_v)], inc=cs)
                            else:
                                ev = r.op("scalar", lambda e, tc=tc, nch=nch, pp=pp: e.activation(
                                    out=Vbf[:, tc * D + (nch - 2) * 512:
                                            tc * D + (nch - 1) * 512],
                                    in_=qkv_ps[pp][:], func=AF.Copy),
                                    waits=[(ct, tkv)], inc=cs)
                                ev_v_tc[tc] = ev
                            ev_qkv[pp] = [(cs, ev)]
                        kvw = Kf[:].rearrange("p (h d) -> p h d", d=DK)
                        r.op("vector", lambda e, kvw=kvw: e.reduce_sum(
                            out=kstat[:, 0:16], in_=kvw, axis=AX.X),
                            waits=[(cs, cs.n)], inc=cv)
                        s_ksq = r.op("scalar", lambda e: e.activation(
                            out=sqtmp[:], in_=Kf[:], func=AF.Square),
                            waits=[(cv, last_ksq_v)], inc=cs)
                        last_ksq_v = r.op("vector", lambda e: e.reduce_sum(
                            out=kstat[:, 16:32], in_=sqtmp[:].rearrange("p (h d) -> p h d", d=DK),
                            axis=AX.X), waits=[(cs, s_ksq)], inc=cv)
                        r.op("vector", lambda e: e.tensor_scalar(
                            out=kstat[:, 32:48], in0=kstat[:, 0:16], scalar1=1.0 / DK,
                            scalar2=None, op0=ALU.mult), inc=cv)
                        r.op("vector", lambda e: e.tensor_tensor(
                            out=kstat[:, 0:16], in0=kstat[:, 32:48], in1=kstat[:, 32:48],
                            op=ALU.mult), inc=cv)
                        r.op("vector", lambda e: e.tensor_scalar(
                            out=kstat[:, 16:32], in0=kstat[:, 16:32], scalar1=1.0 / DK,
                            scalar2=LN_EPS, op0=ALU.mult, op1=ALU.add), inc=cv)
                        v_kv = r.op("vector", lambda e: e.tensor_tensor(
                            out=kstat[:, 16:32], in0=kstat[:, 16:32], in1=kstat[:, 0:16],
                            op=ALU.subtract), inc=cv)
                        s_kstd = r.op("scalar", lambda e: e.activation(
                            out=kstat[:, 16:32], in_=kstat[:, 16:32], func=AF.Sqrt),
                            waits=[(cv, v_kv)], inc=cs)
                        r.op("vector", lambda e: e.reciprocal(
                            out=kstat[:, 48:64], in_=kstat[:, 16:32]),
                            waits=[(cs, s_kstd)], inc=cv)
                        r.op("vector", lambda e, kvw=kvw: e.tensor_tensor(
                            out=kvw, in0=kvw,
                            in1=kstat[:, 32:48].unsqueeze(2).broadcast_to([128, H, DK]),
                            op=ALU.subtract), inc=cv)
                        v_kln_tc[tc] = r.op("vector", lambda e, tc=tc, kvw=kvw: e.tensor_tensor(
                            out=Kbf[:, tc * D:(tc + 1) * D].rearrange("p (h d) -> p h d", d=DK),
                            in0=kvw, in1=kstat[:, 48:64].unsqueeze(2).broadcast_to([128, H, DK]),
                            op=ALU.mult), inc=cv)
                        last_kln_v = v_kln_tc[tc]

                    if attn_stage < 3:
                        continue
                    for c in range(4):
                      if True:
                        tkt = 0
                        for f in range(8):
                            tkt = r.op("tensor", lambda e, c=c, f=f: e.transpose(
                                kt_ps[:, f * 128:(f + 1) * 128],
                                Kbf[:, c * D + f * 128: c * D + (f + 1) * 128], ident_sb[:]),
                                waits=([(cv, v_kln_tc[c]), (cv, last_kt_v),
                                        CST()] if f == 0 else []),
                                inc=(ct if f == 7 else None), selfsync=False)
                        for pb_ in range(2):
                            last_kt_v = r.op("vector", lambda e, pb_=pb_: e.tensor_copy(
                                out=KT_z[pb_ * 64:(pb_ + 1) * 64, :]
                                    .rearrange("p (j two t) -> p j two t", two=2, t=C)
                                    [:, :, pb_:pb_ + 1, :].squeeze(2),
                                in_=kt_ps[pb_ * 64:(pb_ + 1) * 64, :]
                                    .rearrange("p (j t) -> p j t", t=C)),
                                waits=[(ct, tkt)], inc=cv)
                      if attn_stage < 31:
                        continue
                      if True:
                        v_vw = r.op("vector", lambda e, c=c: e.tensor_tensor(
                            out=Vw_ch[:], in0=Vbf[:, c * D:(c + 1) * D], in1=b_full[:],
                            op=ALU.mult), waits=[(cs, ev_v_tc[c]), (cs, decay_s),
                                                 (ct, ct.n)], inc=cv)
                        if attn_stage < 32:
                            continue
                        for hh in range(2):
                            tsc = 0
                            for hi in range(8):
                                h = hh * 8 + hi
                                fc = h // 2
                                pb = (h % 2) * 64
                                tsc = r.op("tensor", lambda e, h=h, fc=fc, hi=hi, c=c:
                                           e.matmul(
                                    sc_ps[:, hi * 128:(hi + 1) * 128],
                                    KT_z[:, h * C:(h + 1) * C],
                                    QT[:, fc * SC + c * 128: fc * SC + (c + 1) * 128],
                                    start=True, stop=True),
                                    waits=([(cv, last_kt_v), (cv, last_A_v)] if hi == 0 else []),
                                    inc=(ct if hi == 7 else None), selfsync=False)
                            if attn_stage < 33:
                                continue
                            wA = [(ct, tsc), (cs, decay_s), (cv, decay_v)]
                            if not first_A_mm:
                                wA.append((ct, last_Y_t))
                            last_A_v = r.op("vector", lambda e, hh=hh: e.tensor_tensor(
                                out=A_bf[:, hh * 8 * C:(hh + 1) * 8 * C],
                                in0=sc_ps[:], in1=DT_all[:, hh * 8 * C:(hh + 1) * 8 * C],
                                op=ALU.mult), waits=wA, inc=cv)
                            first_A_mm = False
                            if attn_stage < 50:
                                continue
                            tY = 0
                            for hi in range(8):
                                h = hh * 8 + hi
                                fc = h // 2
                                pb = (h % 2) * 64
                                j = h // 2
                                tY = r.op("tensor", lambda e, h=h, c=c, pb=pb, j=j: e.matmul(
                                    y_ps[pb:pb + 64, j * 128:(j + 1) * 128],
                                    Vbf[:, c * D + h * DK: c * D + (h + 1) * DK],
                                    A_bf[:, h * C:(h + 1) * C], start=True, stop=False),
                                    waits=([(cv, last_A_v), (cs, last_yevac_s),
                                            (cv, last_state_v)] if hi == 0 else []),
                                    inc=None, selfsync=False)
                                tY = r.op("tensor", lambda e, h=h, c=c, pb=pb, j=j, fc=fc:
                                          e.matmul(
                                    y_ps[pb:pb + 64, j * 128:(j + 1) * 128],
                                    S_z[:, h * DK:(h + 1) * DK],
                                    Qa[:, fc * SC + c * 128: fc * SC + (c + 1) * 128],
                                    start=False, stop=True),
                                    inc=(ct if hi == 7 else None), selfsync=False)
                            last_Y_t = tY
                        if attn_stage < 50:
                            continue
                        last_yevac_s = r.op("scalar", lambda e, c=c: e.activation(
                            out=yT[:, c * 1024:(c + 1) * 1024],
                            in_=y_ps[:], func=AF.Copy),
                            waits=[(ct, last_Y_t)], inc=cs)
                        if attn_stage < 60:
                            continue
                        tst = 0
                        for h in range(H):
                            pb = (h % 2) * 64
                            j = h // 2
                            r.op("tensor", lambda e, h=h, c=c, pb=pb, j=j: e.matmul(
                                st_ps[pb:pb + 64, j * DK:(j + 1) * DK],
                                Kbf[:, c * D + h * DK:c * D + (h + 1) * DK],
                                Vw_ch[:, h * DK:(h + 1) * DK],
                                start=True, stop=False),
                                waits=([(cv, last_state_v), (cv, v_vw)] if h == 0 else []),
                                inc=None, selfsync=False)
                            tst = r.op("tensor", lambda e, h=h, pb=pb, j=j: e.matmul(
                                st_ps[pb:pb + 64, j * DK:(j + 1) * DK],
                                gI_z[:, h * DK:(h + 1) * DK],
                                S_z[:, h * DK:(h + 1) * DK],
                                start=False, stop=True),
                                inc=(ct if h == H - 1 else None), selfsync=False)
                        for pb_ in range(2):
                            last_state_v = r.op("vector", lambda e, pb_=pb_: e.tensor_copy(
                                out=S_z[pb_ * 64:(pb_ + 1) * 64, :]
                                    .rearrange("p (j two d) -> p j two d", two=2, d=DK)
                                    [:, :, pb_:pb_ + 1, :].squeeze(2),
                                in_=st_ps[pb_ * 64:(pb_ + 1) * 64, :]
                                    .rearrange("p (j d) -> p j d", d=DK)),
                                waits=([(ct, tst), (ct, last_Y_t)] if pb_ == 0 else []), inc=cv)
                        if isc + 1 < NSC:
                            emit_ln_tc(isc + 1, c)
                            if c == 3:
                                emit_ln_finish(isc + 1)
                        if attn_stage >= 70:
                            tc = c
                            pp = tc % 2
                            for n in range(2):
                                w0 = [(cs, last_yevac_s), WT] + ev_qkv[pp]
                                if tc == 0 and n == 0:
                                    w0.append((ss, st_s_prev))
                                for j in range(8):
                                    tw = r.op("tensor", lambda e, tc=tc, n=n, j=j, pp=pp:
                                              e.matmul(
                                        qkv_ps[pp][:],
                                        yT[:, tc * 1024 + j * 128: tc * 1024 + (j + 1) * 128],
                                        woT[:, j * D + n * 512: j * D + (n + 1) * 512],
                                        start=(j == 0), stop=(j == 7)),
                                        waits=(w0 if j == 0 else []),
                                        inc=(ct if j == 7 else None), selfsync=False)
                                ev_s = r.op("scalar", lambda e, tc=tc, n=n, pp=pp:
                                            e.activation(
                                    out=s_sb[:, tc * D + n * 512: tc * D + (n + 1) * 512],
                                    in_=qkv_ps[pp][:], func=AF.Copy),
                                    waits=[(ct, tw)], inc=cs)
                                ev_qkv[pp] = [(cs, ev_s)]

                    if attn_stage < 70:
                        continue
                    for tc in range(4):
                        st_s_prev = r.op("sync", lambda e, tc=tc, t0=t0: e.dma_start(
                            out=s_dram[t0 + tc * 128: t0 + (tc + 1) * 128, :],
                            in_=s_sb[:, tc * D:(tc + 1) * D]),
                            waits=[(cs, ev_s)], inc=ss, selfsync=False)
                    free_xnT[buf] = (ct, ct.n)

        phA = dict(t=ct.n, v=cv.n, s=cs.n, sxn=sxn.n, ss=ss.n)

        if do_ffn:
            bstack = contextlib.ExitStack()
            sbB, psB = mk_alloc(bstack)
            with bstack:
                NEE = NE + 4
                xnT_g = [sbB(f"xnT_g{i}", [128, 8 * GT], BF16) for i in range(2)]
                acc2 = [sbB(f"acc{i}", [128, 8 * D]) for i in range(2)]
                upT = [sbB(f"upT{i}", [128, 8 * 512], BF16) for i in range(2)]
                dnT = [sbB(f"dnT{i}", [128, 4 * D], BF16) for i in range(2)]
                hr = sbB("hr", [128, 2048], BF16)
                hp_sb = [sbB(f"hp_sb{i}", [128, 2048], BF16) for i in range(2)]
                gateT = sbB("gateT", [128, 8 * NE], BF16)
                ebias_b = sbB("ebias_b", [128, NE])
                ebrow = sbB("ebrow", [1, NE])
                comb = sbB("comb", [128, 8 * NE])
                rsc_all = sbB("rsc_all", [128, 8 * NE])
                rwork = sbB("rwork", [128, 8 * NE])
                rtmp = sbB("rtmp", [128, 8 * NE])
                rtm2 = sbB("rtm2", [128, 8 * NE])
                pr6 = sbB("pr6", [128, 6 * 64])
                gsco = sbB("gsco", [128, 3 * 64])
                gtmp = sbB("gtmp", [128, 64])
                mx8x = sbB("mx8x", [128, 64])
                mrep4x = sbB("mrep4x", [128, 64])
                mrep6x = sbB("mrep6x", [128, 64])
                rden = sbB("rden", [128, 8])
                gs8 = sbB("gs8", [128, 8 * D], BF16)
                fence_sb = sbB("fence_sb", [1, 8])
                s8 = sbB("s8", [128, 8 * D], BF16)


                h_ps = [psB(f"h_ps{i}", [128, 1024]) for i in range(2)]
                dn_ps = [psB(f"dn_ps{i}", [128, 1024]) for i in range(2)]

                cg = r_cg
                barrier = [(ct, phA["t"]), (cv, phA["v"]), (cs, phA["s"]),
                           (sxn, phA["sxn"]), (ss, phA["ss"]), (lwT, lwT.n)]

                for dc in range(8):
                    r.op("sync", lambda e, dc=dc: e.dma_start(
                        out=gateT[:, dc * NE:(dc + 1) * NE],
                        in_=gate_bf[:, dc * 128:(dc + 1) * 128], transpose=True),
                        waits=[(ldw1, ldw1.n)] + (barrier if dc == 0 else []),
                        inc=lgT, selfsync=False)
                r.op("sync", lambda e: e.dma_start(out=ebrow[:], in_=e_bias),
                     inc=lgT, selfsync=False)
                ld_gateT = ld_eb = (lgT, lgT.n)
                t_eb = r.op("tensor", lambda e: e.matmul(
                    dn_ps[0][:, 0:NE], ones_row[:], ebrow[:], start=True, stop=True),
                    waits=[ld_eb] + barrier, inc=ct, selfsync=False)
                v_eb = r.op("vector", lambda e: e.tensor_copy(
                    out=ebias_b[:], in_=dn_ps[0][:, 0:NE]),
                    waits=[(ct, t_eb)] + barrier, inc=cv)
                r.op("vector", lambda e: e.memset(mrep4x[:], 1e30), inc=cv)
                r.op("vector", lambda e: e.memset(mrep6x[:], 1e30), inc=cv)
                s_bar = r.op("scalar", lambda e: e.activation(
                    out=ebias_b[:, 0:1], in_=ebias_b[:, 0:1], func=AF.Copy),
                    waits=[(cv, v_eb)] + barrier, inc=cs)
                r.op("gpsimd", lambda e: e.memset(fence_sb[:], 0.0),
                     waits=barrier, inc=cg, selfsync=False)

                free_xnTg = [(ct, 0), (ct, 0)]
                free_w = [(ct, 0), (ct, 0)]

                ld_xnTg = [0, 0]
                ld_we = [0, 0]
                last_stt_pp = [(cv, v_eb), (cv, 0)]
                last_relu_s = [0, 0]
                last_hp_v = [0, 0]
                dn_done = [(ct, 0), (ct, 0)]
                hp_ready = [(cv, 0), (cv, 0)]
                router_prev = 0
                s_g_last = 0

                def load_group(g):
                    gb = g % 2
                    for dc in range(8):
                        r.op("sync", lambda e, dc=dc, g=g, gb=gb: e.dma_start(
                            out=xnT_g[gb][:, dc * GT:(dc + 1) * GT],
                            in_=xn_dram[g * GT:(g + 1) * GT, dc * 128:(dc + 1) * 128],
                            transpose=True),
                            waits=[(sxn, phA["sxn"]), free_xnTg[gb]],
                            inc=lxg[gb], selfsync=False)
                    ld_xnTg[gb] = (lxg[gb], lxg[gb].n)

                def load_expert(e_, slot):
                    wprep = (ldw1, ldw1.n)
                    if e_ < NE:
                        for dc in range(8):
                            r.op("sync", lambda e, dc=dc, e_=e_, slot=slot: e.dma_start(
                                out=upT[slot][:, dc * 512:(dc + 1) * 512],
                                in_=up_bf[e_][:, dc * 128:(dc + 1) * 128], transpose=True),
                                waits=[wprep, free_w[slot]], inc=lwe[slot], selfsync=False)
                        for ic in range(4):
                            r.op("sync", lambda e, ic=ic, e_=e_, slot=slot: e.dma_start(
                                out=dnT[slot][:, ic * D:(ic + 1) * D],
                                in_=down_bf[e_][:, ic * 128:(ic + 1) * 128], transpose=True),
                                waits=[wprep], inc=lwe[slot], selfsync=False)
                    else:
                        ep = e_ - NE
                        for dc in range(8):
                            r.op("sync", lambda e, dc=dc, ep=ep, slot=slot: e.dma_start(
                                out=upT[slot][:, dc * 512:(dc + 1) * 512],
                                in_=shup_bf[ep * 512:(ep + 1) * 512, dc * 128:(dc + 1) * 128],
                                transpose=True),
                                waits=[wprep, free_w[slot]], inc=lwe[slot], selfsync=False)
                        for ic in range(4):
                            r.op("sync", lambda e, ic=ic, ep=ep, slot=slot: e.dma_start(
                                out=dnT[slot][:, ic * D:(ic + 1) * D],
                                in_=shdown_bf[:, (ep * 4 + ic) * 128:(ep * 4 + ic + 1) * 128],
                                transpose=True),
                                waits=[wprep], inc=lwe[slot], selfsync=False)
                    ld_we[slot] = (lwe[slot], lwe[slot].n)

                load_group(0)
                if NGRP > 1:
                    load_group(1)
                load_expert(NE, 0)
                load_expert(NE + 1, 1)

                fin_done = [(cv, 0), (cv, 0)]
                pend_fin = None
                free_s8 = (cv, 0)
                free_gs8 = (cs, 0)
                def vfin_cur():
                    return max(last_stt_pp[0][1], last_stt_pp[1][1])

                def emit_finalize(gf):
                    nonlocal fin_done, free_gs8
                    g_, vfin = gf
                    acc = acc2[g_ % 2]
                    for i8 in range(8):
                        ls = r.op("gpsimd", lambda e, g_=g_, i8=i8: e.dma_start(
                            out=s8[:, i8 * D:(i8 + 1) * D],
                            in_=s_dram[g_ * GT + i8 * 128: g_ * GT + (i8 + 1) * 128, :]),
                            waits=([(ss, phA["ss"]), free_s8] if i8 == 0 else []),
                            inc=lsl[0], selfsync=False)
                    ls = (lsl[0], lsl[0].n)
                    s_all = r.op("scalar", lambda e, acc=acc: e.activation(
                        out=gs8[:], in_=acc[:], func=AF.Sigmoid),
                        waits=[(cv, vfin), free_gs8], inc=cs)
                    v_g8 = r.op("vector", lambda e, acc=acc: e.tensor_tensor(
                        out=gs8[:], in0=acc[:], in1=gs8[:], op=ALU.mult),
                        waits=[(cs, s_all)], inc=cv)
                    fin_done[g_ % 2] = (cv, v_g8)
                    v_sg = r.op("vector", lambda e: e.tensor_tensor(
                        out=gs8[:], in0=s8[:], in1=gs8[:], op=ALU.mult),
                        waits=[ls], inc=cv)
                    free_s8_v = (cv, v_sg)
                    for idx in range(8):
                        row0 = g_ * GT + idx * 128
                        b = row0 // L
                        l0 = row0 % L
                        r.op("gpsimd", lambda e, b=b, l0=l0, idx=idx: e.dma_start(
                            out=out[b, l0:l0 + 128, :],
                            in_=gs8[:, idx * D:(idx + 1) * D], accum_op=ALU.add),
                            waits=([(cv, v_sg), (pf, pf.n)] if idx == 0 else []),
                            inc=sto[0], selfsync=False)
                    free_gs8 = (sto[0], sto[0].n)
                    return free_s8_v


                for g in range(NGRP):
                    gb = g % 2
                    acc = acc2[g % 2]
                    v_z = 0
                    for i in range(8):
                        v_z = r.op("vector", lambda e, i=i, acc=acc: e.memset(
                            acc[:, i * D:(i + 1) * D], 0.0),
                            waits=([fin_done[g % 2]] if i == 0 else []), inc=cv)
                    s_sig2 = 0
                    for j2 in range(2):
                        tl = 0
                        for tl4 in range(4):
                            w0 = ([ld_xnTg[gb], ld_gateT,
                                   (cs, last_relu_s[j2]), (cg, last_hp_v[j2])]
                                  if tl4 == 0 else [])
                            for dc in range(8):
                                tl = r.op("tensor", lambda e, j2=j2, tl4=tl4, dc=dc, gb=gb:
                                          e.matmul(
                                    h_ps[j2][:, tl4 * 32:(tl4 + 1) * 32],
                                    xnT_g[gb][:, dc * GT + (j2 * 4 + tl4) * 128:
                                               dc * GT + (j2 * 4 + tl4 + 1) * 128],
                                    gateT[:, dc * NE:(dc + 1) * NE],
                                    start=(dc == 0), stop=(dc == 7)),
                                    waits=(w0 if dc == 0 else []),
                                    inc=(ct if (tl4 == 3 and dc == 7) else None),
                                    selfsync=False)
                        s_sig2 = r.op("scalar", lambda e, j2=j2: e.activation(
                            out=rsc_all[:, j2 * 128:(j2 + 1) * 128],
                            in_=h_ps[j2][:, 0:128], func=AF.Sigmoid),
                            waits=[(ct, tl), (cv, router_prev)], inc=cs)
                        last_relu_s[j2] = s_sig2

                    def vst(fn, waits=()):
                        return r.op("vector", fn, waits=waits, inc=cv, selfsync=False)
                    rw4 = rwork[:].rearrange("p (i g m) -> p i g m", g=8, m=4)
                    st = vst(lambda e: e.tensor_tensor(
                        out=rwork[:].rearrange("p (i e) -> p i e", e=NE),
                        in0=rsc_all[:].rearrange("p (i e) -> p i e", e=NE),
                        in1=ebias_b[:].unsqueeze(1).broadcast_to([128, 8, NE]),
                        op=ALU.add), waits=[(cs, s_sig2), (cv, v_eb), (cv, cv.n)])
                    pairs = [(0, 1), (2, 3), (0, 2), (1, 3), (0, 3), (1, 2)]
                    for pi, (pa, pb_) in enumerate(pairs):
                        st2 = vst(lambda e, pa=pa, pb_=pb_, pi=pi: e.tensor_tensor(
                            out=pr6[:, pi * 64:(pi + 1) * 64],
                            in0=rw4[:, :, :, pa:pa + 1].squeeze(3),
                            in1=rw4[:, :, :, pb_:pb_ + 1].squeeze(3), op=ALU.add),
                            waits=([(cv, st)] if pi == 0 else []))
                    st3 = None
                    for (o, a, b_) in ((0, 0, 1), (1, 2, 3), (2, 4, 5)):
                        st3 = vst(lambda e, o=o, a=a, b_=b_: e.tensor_tensor(
                            out=gsco[:, o * 64:(o + 1) * 64], in0=pr6[:, a * 64:(a + 1) * 64],
                            in1=pr6[:, b_ * 64:(b_ + 1) * 64], op=ALU.max),
                            waits=([(cv, st2)] if o == 0 else []))
                    st4 = vst(lambda e: e.tensor_tensor(
                        out=gsco[:, 0:64], in0=gsco[:, 0:64], in1=gsco[:, 64:128],
                        op=ALU.max), waits=[(cv, st3)])
                    st5 = vst(lambda e: e.tensor_tensor(
                        out=gsco[:, 0:64], in0=gsco[:, 0:64], in1=gsco[:, 128:192],
                        op=ALU.max), waits=[(cv, st4)])
                    st6 = None
                    for s_ in range(8):
                        st6 = vst(lambda e, s_=s_: e.max(
                            out=mx8x[:, s_ * 8:(s_ + 1) * 8], in_=gsco[:, s_ * 8:(s_ + 1) * 8]),
                            waits=([(cv, st5)] if s_ == 0 else []))
                    st7 = None
                    for s_ in range(8):
                        st7 = vst(lambda e, s_=s_: e.tensor_copy(
                            out=mrep4x[:, s_ * 8:s_ * 8 + 4], in_=mx8x[:, s_ * 8:s_ * 8 + 4]),
                            waits=([(cv, st6)] if s_ == 0 else []))
                    st8 = None
                    for s_ in range(8):
                        st8 = vst(lambda e, s_=s_: e.match_replace(
                            out=gtmp[:, s_ * 8:(s_ + 1) * 8], in_to_replace=mrep4x[:, s_ * 8:(s_ + 1) * 8],
                            in_values=gsco[:, s_ * 8:(s_ + 1) * 8], imm_value=-1e30),
                            waits=([(cv, st7)] if s_ == 0 else []))
                    st9 = vst(lambda e: e.tensor_scalar(
                        out=gtmp[:], in0=gtmp[:], scalar1=-1e29, scalar2=None,
                        op0=ALU.is_lt), waits=[(cv, st8)])
                    st10 = vst(lambda e: e.tensor_tensor(
                        out=rtmp[:].rearrange("p (i g m) -> p i g m", g=8, m=4),
                        in0=rw4,
                        in1=gtmp[:].rearrange("p (i g) -> p i g", g=8).unsqueeze(3)
                            .broadcast_to([128, 8, 8, 4]), op=ALU.mult),
                        waits=[(cv, st9)])
                    st11 = None
                    for s_ in range(8):
                        st11 = vst(lambda e, s_=s_: e.max(
                            out=mx8x[:, s_ * 8:(s_ + 1) * 8], in_=rtmp[:, s_ * 32:(s_ + 1) * 32]),
                            waits=([(cv, st10)] if s_ == 0 else []))
                    st12 = None
                    for s_ in range(8):
                        st12 = vst(lambda e, s_=s_: e.tensor_copy(
                            out=mrep6x[:, s_ * 8:s_ * 8 + 6], in_=mx8x[:, s_ * 8:s_ * 8 + 6]),
                            waits=([(cv, st11)] if s_ == 0 else []))
                    st13 = None
                    for s_ in range(8):
                        st13 = vst(lambda e, s_=s_: e.match_replace(
                            out=rtm2[:, s_ * 32:(s_ + 1) * 32], in_to_replace=mrep6x[:, s_ * 8:(s_ + 1) * 8],
                            in_values=rtmp[:, s_ * 32:(s_ + 1) * 32], imm_value=-1e30),
                            waits=([(cv, st12)] if s_ == 0 else []))
                    st14 = vst(lambda e: e.tensor_scalar(
                        out=rtm2[:], in0=rtm2[:], scalar1=-1e29, scalar2=None,
                        op0=ALU.is_lt), waits=[(cv, st13)])
                    st15 = vst(lambda e: e.tensor_tensor(
                        out=rwork[:], in0=rtm2[:], in1=rsc_all[:], op=ALU.mult),
                        waits=[(cv, st14)])
                    st16 = vst(lambda e: e.reduce_sum(
                        out=rden[:], in_=rwork[:].rearrange("p (i e) -> p i e", e=NE),
                        axis=AX.X), waits=[(cv, st15)])
                    st17 = vst(lambda e: e.tensor_scalar(
                        out=rden[:], in0=rden[:], scalar1=1e-20,
                        scalar2=1.0 / ROUTED_SCALE, op0=ALU.add, op1=ALU.mult),
                        waits=[(cv, st16)])
                    st18 = vst(lambda e: e.reciprocal(out=rden[:], in_=rden[:]),
                               waits=[(cv, st17)])
                    v_cmb_last = vst(lambda e: e.tensor_tensor(
                        out=comb[:].rearrange("p (i e) -> p i e", e=NE),
                        in0=rwork[:].rearrange("p (i e) -> p i e", e=NE),
                        in1=rden[:].unsqueeze(2).broadcast_to([128, 8, NE]),
                        op=ALU.mult), waits=[(cv, st18)])
                    router_prev = v_cmb_last
                    v_router = v_cmb_last

                    NU = GT // 512
                    eorder = list(range(NE, NEE)) + list(range(NE))
                    units = [(e_, u) for e_ in eorder for u in range(NU)]

                    def emit_up(k):
                        e_, u = units[k]
                        slot = (k // NU) % 2
                        kb = k % 2
                        for hp in range(2):
                            tu = 0
                            for ic2 in range(2):
                                ic = hp * 2 + ic2
                                w0 = ([ld_we[slot], ld_xnTg[gb],
                                       (cs, last_relu_s[hp])]
                                      if ic2 == 0 else [])
                                for dc in range(8):
                                    tu = r.op("tensor", lambda e, ic=ic, dc=dc, u=u, gb=gb,
                                              slot=slot, hp=hp, ic2=ic2: e.matmul(
                                        h_ps[hp][:, ic2 * 512:(ic2 + 1) * 512],
                                        upT[slot][:, dc * 512 + ic * 128:
                                                  dc * 512 + (ic + 1) * 128],
                                        xnT_g[gb][:, dc * GT + u * 512:
                                                  dc * GT + (u + 1) * 512],
                                        start=(dc == 0), stop=(dc == 7)),
                                        waits=(w0 if dc == 0 else []),
                                        inc=(ct if (ic2 == 1 and dc == 7) else None),
                                        selfsync=False)
                            last_relu_s[hp] = r.op("scalar", lambda e, hp=hp: e.activation(
                                out=hr[:, hp * 1024:(hp + 1) * 1024], in_=h_ps[hp][:],
                                func=AF.Relu),
                                waits=[(ct, tu), (cg, last_hp_v[hp])], inc=cs)
                            last_hp_v[hp] = r.op("gpsimd", lambda e, hp=hp, kb=kb:
                                                 e.tensor_tensor(
                                out=hp_sb[kb][:, hp * 1024:(hp + 1) * 1024],
                                in0=hr[:, hp * 1024:(hp + 1) * 1024],
                                in1=hr[:, hp * 1024:(hp + 1) * 1024], op=ALU.mult),
                                waits=[(cs, last_relu_s[hp]), dn_done[kb]],
                                inc=cg, selfsync=True)
                        hp_ready[kb] = (cg, last_hp_v[1])

                    def emit_dn(k):
                        e_, u = units[k]
                        slot = (k // NU) % 2
                        kb = k % 2
                        last_td = 0
                        for tcl in range(4):
                            pp = tcl % 2
                            td = 0
                            w0 = [hp_ready[kb], last_stt_pp[pp]]
                            for n in range(2):
                                for ic in range(4):
                                    td = r.op("tensor", lambda e, tcl=tcl, n=n, ic=ic,
                                              slot=slot, pp=pp, kb=kb: e.matmul(
                                        dn_ps[pp][:, n * 512:(n + 1) * 512],
                                        hp_sb[kb][:, ic * 512 + tcl * 128:
                                              ic * 512 + (tcl + 1) * 128],
                                        dnT[slot][:, ic * D + n * 512:
                                                 ic * D + (n + 1) * 512],
                                        start=(ic == 0), stop=(ic == 3)),
                                        waits=(w0 if (n == 0 and ic == 0) else []),
                                        inc=(ct if (n == 1 and ic == 3) else None),
                                        selfsync=False)
                            last_td = td
                            idx = u * 4 + tcl
                            acc_ = acc2[g % 2]
                            if e_ < NE:
                                vv = r.op("vector", lambda e, idx=idx, e_=e_, pp=pp, acc_=acc_:
                                          e.scalar_tensor_tensor(
                                    out=acc_[:, idx * D:(idx + 1) * D], in0=dn_ps[pp][:],
                                    scalar=comb[:, idx * NE + e_: idx * NE + e_ + 1],
                                    in1=acc_[:, idx * D:(idx + 1) * D],
                                    op0=ALU.mult, op1=ALU.add),
                                    waits=[(ct, td), (cv, v_router), (cv, v_z)], inc=cv)
                            else:
                                vv = r.op("vector", lambda e, idx=idx, pp=pp, acc_=acc_:
                                          e.tensor_tensor(
                                    out=acc_[:, idx * D:(idx + 1) * D], in0=dn_ps[pp][:],
                                    in1=acc_[:, idx * D:(idx + 1) * D], op=ALU.add),
                                    waits=[(ct, td), (cv, v_z)], inc=cv)
                            last_stt_pp[pp] = (cv, vv)
                        dn_done[kb] = (ct, last_td)
                        if u == NU - 1:
                            free_w[slot] = (ct, last_td)
                            pos = k // NU
                            nxt = pos + 2
                            if nxt < NEE:
                                load_expert(eorder[nxt], nxt % 2)
                            elif g + 1 < NGRP:
                                load_expert(eorder[nxt - NEE], (nxt - NEE) % 2)

                    emit_up(0)
                    for k in range(len(units)):
                        if k + 1 < len(units):
                            emit_up(k + 1)
                        emit_dn(k)
                        if k == 3 and pend_fin is not None:
                            free_s8 = emit_finalize(pend_fin)
                            pend_fin = None
                    free_xnTg[gb] = (ct, ct.n)
                    if g + 2 < NGRP:
                        load_group(g + 2)
                    pend_fin = (g, vfin_cur())

                if pend_fin is not None:
                    free_s8 = emit_finalize(pend_fin)
                    pend_fin = None

        final_waits = [(sto[0], sto[0].n), (sto[1], sto[1].n)]
        for c_ in [ldw0, ldw0b, ldw1, lcst, lwT, lxA[0], lxA[1], lxnT[0], lxnT[1],
                   sxn, ss, lgT, lxg[0], lxg[1], lwe[0], lwe[1], lxl[0], lxl[1],
                   lsl[0], lsl[1], lcw[0], lcw[1], pf]:
            if c_.n > 0:
                final_waits.append((c_, c_.n))
        r.op("sync", lambda e: None, waits=final_waits, selfsync=False)

        block = ostack.enter_context(nc.Block())
        r.flush(block)

    return nc


def split_inputs(inp, n_cores=8):
    xs = np.ascontiguousarray(inp["L0_attn_x"])
    nb = xs.shape[0] // n_cores
    hc = host_consts()
    maps = []
    for i in range(n_cores):
        m = dict(
            x=np.ascontiguousarray(xs[i * nb:(i + 1) * nb]),
            w_qkv=inp["L0_attn_W_qkv"],
            w_o=inp["L0_attn_W_o"],
            log_gamma=np.asarray(inp["L0_attn_log_gamma"]).reshape(1, H),
            gate_w=inp["L0_ffn_gate_w"],
            e_bias=np.asarray(inp["L0_ffn_e_bias"]).reshape(1, NE),
            up_w=inp["L0_ffn_up_w"],
            down_w=inp["L0_ffn_down_w"],
            shup_w=inp["L0_ffn_shared_up_w"],
            shdown_w=inp["L0_ffn_shared_down_w"],
        )
        m.update(hc)
        maps.append({k: np.ascontiguousarray(np.asarray(v)) for k, v in m.items()})
    return maps


_NC_CACHE = {}


def kernel(**inputs):
    n_cores = 8
    xs = np.ascontiguousarray(np.asarray(inputs["L0_attn_x"], dtype=np.float32))
    nb = xs.shape[0] // n_cores
    if "nc" not in _NC_CACHE:
        _NC_CACHE["nc"] = build(n_batches=nb)
    nc = _NC_CACHE["nc"]
    maps = split_inputs(inputs, n_cores=n_cores)
    from concourse.bass_utils import run_bass_kernel_spmd
    res = run_bass_kernel_spmd(nc, maps, core_ids=list(range(n_cores)))
    out = np.concatenate([np.asarray(r["out"]) for r in res.results], axis=0)
    return out


# version 20 (final)
# speedup vs baseline: 1.2124x; 1.1045x over previous
import contextlib
import numpy as np
import concourse.bass as bass
import concourse.mybir as mybir

F32 = mybir.dt.float32
BF16 = mybir.dt.bfloat16
AF = mybir.ActivationFunctionType
ALU = mybir.AluOpType
AX = mybir.AxisListType

D = 1024
H = 16
DK = 64
C = 128
SC = 512
NE = 32
EI = 512
SI = 2048
LN_EPS = 1e-5
GT = 1024
ROUTED_SCALE = 2.5
L = 2048

ENGINES = ("sync", "tensor", "vector", "scalar", "gpsimd")


def host_consts():
    import ml_dtypes
    t = np.arange(C, dtype=np.float32)
    deltaT = np.maximum(t[None, :] - t[:, None], 0.0).astype(np.float32)
    causalT = (t[None, :] >= t[:, None]).astype(np.float32)
    tmod1 = np.broadcast_to(t + 1.0, (C, C)).astype(np.float32).copy()
    revcol = (C - 1 - t).reshape(C, 1).astype(np.float32)
    ident_bf = np.eye(C).astype(ml_dtypes.bfloat16)
    return dict(c_deltaT=deltaT, c_causalT=causalT, c_tmod1=tmod1,
                c_revcol=revcol, c_ident=ident_bf)


class Ctr:
    def __init__(self, sem, step=1):
        self.sem = sem
        self.n = 0
        self.step = step


class Rec:
    def __init__(self):
        self.ops = {e: [] for e in ENGINES}

    def op(self, eng, fn, waits=(), inc=None, selfsync=None):
        wl = []
        for (c, v) in waits:
            assert v <= c.n, f"wait on future value {v} > {c.n}"
            if v > 0:
                wl.append((c.sem, v))
        if selfsync is None:
            selfsync = eng in ("vector", "scalar")
        post = None
        if inc is not None:
            if selfsync and inc.n > 0:
                wl.append((inc.sem, inc.n))
            inc.n += inc.step
            post = inc.n
        self.ops[eng].append((wl, fn, (inc.sem, inc.step) if inc else None))
        return post

    def flush(self, block):
        for eng in ENGINES:
            entries = self.ops[eng]
            if not entries:
                continue

            def make(entries):
                def run(e):
                    for (wl, fn, inc) in entries:
                        for (sem, v) in wl:
                            e.wait_ge(sem, v)
                        inst = fn(e)
                        if inc is not None:
                            assert inst is not None
                            inst.then_inc(inc[0], inc[1])
                return run
            getattr(block, eng)(make(entries))


def build(n_batches=4, debug=False, do_attn=True, do_ffn=True,
          n_sc_limit=None, n_grp_limit=None, attn_stage=99):
    T_core = n_batches * L
    NSC = (T_core // SC) if n_sc_limit is None else n_sc_limit
    NGRP = (T_core // GT) if n_grp_limit is None else n_grp_limit
    nc = bass.Bass("TRN2", target_bir_lowering=False)

    x_in = nc.dram_tensor("x", [n_batches, L, D], F32, kind="ExternalInput").ap()
    w_qkv = nc.dram_tensor("w_qkv", [3 * D, D], F32, kind="ExternalInput").ap()
    w_o = nc.dram_tensor("w_o", [D, D], F32, kind="ExternalInput").ap()
    log_gamma = nc.dram_tensor("log_gamma", [1, H], F32, kind="ExternalInput").ap()
    gate_w = nc.dram_tensor("gate_w", [NE, D], F32, kind="ExternalInput").ap()
    e_bias = nc.dram_tensor("e_bias", [1, NE], F32, kind="ExternalInput").ap()
    up_w = nc.dram_tensor("up_w", [NE, EI, D], F32, kind="ExternalInput").ap()
    down_w = nc.dram_tensor("down_w", [NE, D, EI], F32, kind="ExternalInput").ap()
    shup_w = nc.dram_tensor("shup_w", [SI, D], F32, kind="ExternalInput").ap()
    shdown_w = nc.dram_tensor("shdown_w", [D, SI], F32, kind="ExternalInput").ap()
    hc = host_consts()
    cst = {}
    for n, a in hc.items():
        dt = BF16 if a.dtype.name == "bfloat16" else F32
        cst[n] = nc.dram_tensor(n, list(a.shape), dt, kind="ExternalInput").ap()

    out = nc.dram_tensor("out", [n_batches, L, D], F32, kind="ExternalOutput").ap()

    wqkv_bf = nc.dram_tensor("wqkv_bf", [3 * D, D], BF16, kind="Internal").ap()
    wo_bf = nc.dram_tensor("wo_bf", [D, D], BF16, kind="Internal").ap()
    gate_bf = nc.dram_tensor("gate_bf", [NE, D], BF16, kind="Internal").ap()
    up_bf = nc.dram_tensor("up_bf", [NE, EI, D], BF16, kind="Internal").ap()
    down_bf = nc.dram_tensor("down_bf", [NE, D, EI], BF16, kind="Internal").ap()
    shup_bf = nc.dram_tensor("shup_bf", [SI, D], BF16, kind="Internal").ap()
    shdown_bf = nc.dram_tensor("shdown_bf", [D, SI], BF16, kind="Internal").ap()
    up_bfT = nc.dram_tensor("up_bfT", [NE, D, EI], BF16, kind="Internal").ap()
    down_bfT = nc.dram_tensor("down_bfT", [NE, EI, D], BF16, kind="Internal").ap()
    shupT_d = nc.dram_tensor("shupT_d", [D, SI], BF16, kind="Internal").ap()
    shdownT_d = nc.dram_tensor("shdownT_d", [SI, D], BF16, kind="Internal").ap()
    xnT_dram = nc.dram_tensor("xnT_scratch", [D, T_core], BF16, kind="Internal").ap()
    dkind = "ExternalOutput" if debug else "Internal"
    xn_dram = nc.dram_tensor("xn_scratch", [T_core, D], BF16, kind=dkind).ap()
    s_dram = nc.dram_tensor("s_scratch", [T_core, D], BF16, kind=dkind).ap()

    ostack = contextlib.ExitStack()
    with ostack:
        def mk_alloc(stk):
            def sb(name, shape, dt=F32):
                return stk.enter_context(nc.sbuf_tensor(name, shape, dt))
            def psb(name, shape, dt=F32):
                return stk.enter_context(nc.psum_tensor(name, shape, dt))
            return sb, psb
        sb, psb = mk_alloc(ostack)
        mksem = lambda name: ostack.enter_context(nc.semaphore(name))

        r = Rec()
        ct = Ctr(mksem("ct"))
        cv = Ctr(mksem("cv"))
        cs = Ctr(mksem("cs"))
        r_cg = Ctr(mksem("cg"))
        ldw0 = Ctr(mksem("ldw0"), 16)
        ldw0b = Ctr(mksem("ldw0b"), 16)
        ldw1 = Ctr(mksem("ldw1"), 16)
        lcst = Ctr(mksem("lcst"), 16)
        lwT = Ctr(mksem("lwT"), 16)
        lxA = [Ctr(mksem("lxA0"), 16), Ctr(mksem("lxA1"), 16)]
        lxnT = [Ctr(mksem("lxnT0"), 16), Ctr(mksem("lxnT1"), 16)]
        sxn = Ctr(mksem("sxn"), 16)
        sxnT = Ctr(mksem("sxnT"), 16)
        ss = Ctr(mksem("ss"), 16)
        lgT = Ctr(mksem("lgT"), 16)
        lxg = [Ctr(mksem("lxg0"), 16), Ctr(mksem("lxg1"), 16)]
        lwe = [Ctr(mksem("lwe0"), 16), Ctr(mksem("lwe1"), 16)]
        lxl = [Ctr(mksem("lxl0"), 16), Ctr(mksem("lxl1"), 16)]
        lsl = [Ctr(mksem("lsl0"), 16), Ctr(mksem("lsl1"), 16)]
        sto = [Ctr(mksem("sto0"), 16), Ctr(mksem("sto1"), 16)]

        ones_row = sb("ones_row", [1, 128])
        lgb = sb("lgb", [128, H])
        cstage = [sb(f"cstage{i}", [128, 2048]) for i in range(2)]
        cbf = [sb(f"cbf{i}", [128, 2048], BF16) for i in range(2)]

        cast_jobs = []

        def cast_dma(dst, srcw, ctr):
            cast_jobs.append((dst, srcw, ctr))

        cast_dma(wqkv_bf[:, 0:512], w_qkv[:, 0:512], ldw0)
        cast_dma(wqkv_bf[:, 512:1024], w_qkv[:, 512:1024], ldw0b)
        cast_dma(wo_bf, w_o, ldw0)
        cast_dma(gate_bf, gate_w, ldw1)
        cast_dma(shup_bf, shup_w, ldw1)
        cast_dma(shdown_bf, shdown_w, ldw1)
        for e_ in range(NE):
            cast_dma(up_bf[e_], up_w[e_], ldw1)
            cast_dma(down_bf[e_], down_w[e_], ldw1)

        pf = Ctr(mksem("pf"), 16)

        astack = contextlib.ExitStack()
        sbA, psA = mk_alloc(astack)
        ident_sb = sbA("ident_sb", [128, 128], BF16)
        DT_all = sbA("DT_all", [128, H * C])
        a_sm = sbA("a_sm", [128, 8 * C], BF16)
        b_full = sbA("b_full", [128, D])
        gI_z = sbA("gI_z", [128, H * DK], BF16)
        pstack = contextlib.ExitStack()
        sbP, _psP = mk_alloc(pstack)
        deltaT_sb = sbP("deltaT_sb", [128, 128])
        causalT_sb = sbP("causalT_sb", [128, 128])
        tmod1_sb = sbP("tmod1_sb", [128, 128])
        revcol_sb = sbP("revcol_sb", [128, 1])
        lgrow = sbP("lgrow", [1, H])
        lgtmp = sbP("lgtmp", [1, H])
        atmp = sbP("atmp", [128, 8 * C])
        btmp = sbP("btmp", [128, D])
        gcs = sbP("gcs", [128, H])
        lgQ = sbP("lgQ", [128, 8])

        dstm = dict(c_deltaT=deltaT_sb, c_causalT=causalT_sb, c_tmod1=tmod1_sb,
                    c_revcol=revcol_sb, c_ident=ident_sb)
        for n in dstm:
            r.op("sync", lambda e, d=dstm[n], s=cst[n]: e.dma_start(out=d[:], in_=s),
                 inc=lcst, selfsync=False)
        r.op("sync", lambda e: e.dma_start(out=lgrow[:], in_=log_gamma),
             inc=lcst, selfsync=False)
        CST = lambda: (lcst, lcst.n)

        lcw = [Ctr(mksem("lcw0"), 16), Ctr(mksem("lcw1"), 16)]
        ccg = Ctr(mksem("ccg"))
        jobs_done = []
        cast_final = [None, None]

        def emit_casts():
            chunks = []
            for (dstw, srcw, ctr) in cast_jobs:
                R, W = srcw.shape
                rows = (2048 * 128 // W) if W <= 2048 else 0
                assert rows > 0 and 2048 % W == 0 and dstw.shape == (R, W)
                r0 = 0
                while r0 < R:
                    nr = min(rows, R - r0)
                    if nr % 128 == 0:
                        sv = srcw[r0:r0 + nr, :].rearrange("(c p) w -> p c w", p=128)
                        dv = dstw[r0:r0 + nr, :].rearrange("(c p) w -> p c w", p=128)
                        pshape, fw = 128, (nr // 128) * W
                    else:
                        sv = srcw[r0:r0 + nr, :]
                        dv = dstw[r0:r0 + nr, :]
                        pshape, fw = nr, W
                    chunks.append((sv, dv, pshape, fw, W, ctr, r0 + nr >= R))
                    r0 += nr

            prev_copy = [0, 0]
            st_free = [None, None]
            ldval = {}

            def dma_view(buf, sl, pshape, fw, W):
                v = buf[sl][0:pshape, 0:fw]
                if pshape == 128 and fw != W:
                    v = v.rearrange("p (c w) -> p c w", w=W)
                return v

            def emit_load(k):
                sv, dv, pshape, fw, W, ctr, _ = chunks[k]
                sl = k % 2
                stg = dma_view(cstage, sl, pshape, fw, W)
                w = [(ccg, prev_copy[sl])] if prev_copy[sl] else []
                r.op("gpsimd", lambda e, sv=sv, stg=stg: e.dma_start(
                    out=stg, in_=sv), waits=w, inc=lcw[sl], selfsync=False)
                ldval[k] = lcw[sl].n

            emit_load(0)
            if len(chunks) > 1:
                emit_load(1)
            for k in range(len(chunks)):
                sv, dv, pshape, fw, W, ctr, jend = chunks[k]
                sl = k % 2
                wl = [(lcw[sl], ldval[k])]
                if st_free[sl] is not None:
                    wl.append((st_free[sl][0], st_free[sl][0].n))
                c = r.op("gpsimd", lambda e, sl=sl, pshape=pshape, fw=fw:
                         e.tensor_copy(out=cbf[sl][0:pshape, 0:fw],
                                       in_=cstage[sl][0:pshape, 0:fw]),
                         waits=wl, inc=ccg, selfsync=False)
                prev_copy[sl] = c
                sbf = dma_view(cbf, sl, pshape, fw, W)
                r.op("gpsimd", lambda e, dv=dv, sbf=sbf: e.dma_start(
                    out=dv, in_=sbf), waits=[(ccg, c)], inc=ctr, selfsync=False)
                st_free[sl] = (ctr, ctr.n)
                if jend:
                    jobs_done.append((ctr, ctr.n))
                if k + 2 < len(chunks):
                    emit_load(k + 2)
            cast_final[0] = st_free[0]
            cast_final[1] = st_free[1]

        v_ones = r.op("vector", lambda e: e.memset(ones_row[:], 1.0), inc=cv)
        s1 = r.op("scalar", lambda e: e.activation(out=lgtmp[:], in_=lgrow[:], func=AF.Exp,
                                                   scale=-1.0), waits=[CST()], inc=cs)
        v1 = r.op("vector", lambda e: e.tensor_scalar(out=lgtmp[:], in0=lgtmp[:], scalar1=1e30,
                                                      scalar2=None, op0=ALU.min),
                  waits=[(cs, s1)], inc=cv)
        s2 = r.op("scalar", lambda e: e.activation(out=lgtmp[:], in_=lgtmp[:], func=AF.Ln,
                                                   bias=1.0), waits=[(cv, v1)], inc=cs)
        v2 = r.op("vector", lambda e: e.tensor_scalar(out=lgrow[:], in0=lgtmp[:], scalar1=-1.0,
                                                      scalar2=None, op0=ALU.mult),
                  waits=[(cs, s2)], inc=cv)
        with nc.psum_tensor("lg_ps", [128, 512], F32) as lg_ps_t:
            lg_ps = lg_ps_t.ap()
            t1 = r.op("tensor", lambda e: e.matmul(lg_ps[:, 0:H], ones_row[:], lgrow[:],
                                                   start=True, stop=True),
                      waits=[(cv, v2), (cv, v_ones)], inc=ct)
            v_lgb = r.op("vector", lambda e: e.tensor_copy(out=lgb[:], in_=lg_ps[:, 0:H]),
                         waits=[(ct, t1)], inc=cv)
        v4 = r.op("vector", lambda e: e.tensor_scalar(
            out=btmp[:].rearrange("p (h d) -> p h d", d=DK),
            in0=lgb[:].unsqueeze(2).broadcast_to([128, H, DK]),
            scalar1=revcol_sb[:, 0:1], scalar2=None, op0=ALU.mult),
            waits=[(cv, v_lgb), CST()], inc=cv)
        r.op("scalar", lambda e: e.activation(out=b_full[:], in_=btmp[:], func=AF.Exp),
             waits=[(cv, v4)], inc=cs)
        v5 = r.op("vector", lambda e: e.tensor_scalar(out=gcs[:], in0=lgb[:], scalar1=float(C),
                                                      scalar2=None, op0=ALU.mult),
                  waits=[(cv, v_lgb)], inc=cv)
        sgc = r.op("scalar", lambda e: e.activation(out=gcs[:], in_=gcs[:], func=AF.Exp),
                   waits=[(cv, v5)], inc=cs)
        r.op("vector", lambda e: e.memset(gI_z[:], 0.0), inc=cv)
        for pb in range(2):
            r.op("vector", lambda e, pb=pb: e.tensor_tensor(
                out=gI_z[pb * 64:(pb + 1) * 64, :]
                    .rearrange("p (j two d) -> p j two d", two=2, d=DK)
                    [:, :, pb:pb + 1, :].squeeze(2),
                in0=ident_sb[pb * 64:(pb + 1) * 64, pb * 64:(pb + 1) * 64]
                    .unsqueeze(1).broadcast_to([64, 8, DK]),
                in1=gcs[pb * 64:(pb + 1) * 64, pb:H:2].unsqueeze(2).broadcast_to([64, 8, DK]),
                op=ALU.mult),
                waits=[(cs, sgc), CST()], inc=cv)
            r.op("vector", lambda e, pb=pb: e.tensor_copy(
                out=lgQ[pb * 64:(pb + 1) * 64, :],
                in_=lgb[pb * 64:(pb + 1) * 64, pb:H:2]),
                waits=[(cv, v_lgb)], inc=cv)
        v_lgq = cv.n
        vd1 = r.op("vector", lambda e: e.tensor_tensor(
            out=DT_all[:].rearrange("p (h t) -> p h t", t=C),
            in0=deltaT_sb[:].unsqueeze(1).broadcast_to([128, H, C]),
            in1=lgb[:].unsqueeze(2).broadcast_to([128, H, C]), op=ALU.mult),
            waits=[(cv, v_lgb), CST()], inc=cv)
        sd1 = r.op("scalar", lambda e: e.activation(out=DT_all[:], in_=DT_all[:], func=AF.Exp),
                   waits=[(cv, vd1)], inc=cs)
        r.op("vector", lambda e: e.tensor_tensor(
            out=DT_all[:].rearrange("p (h t) -> p h t", t=C),
            in0=DT_all[:].rearrange("p (h t) -> p h t", t=C),
            in1=causalT_sb[:].unsqueeze(1).broadcast_to([128, H, C]), op=ALU.mult),
            waits=[(cs, sd1), CST()], inc=cv)
        va = r.op("vector", lambda e: e.tensor_tensor(
            out=atmp[:].rearrange("p (f t) -> p f t", t=C),
            in0=tmod1_sb[:].unsqueeze(1).broadcast_to([128, 8, C]),
            in1=lgQ[:].unsqueeze(2).broadcast_to([128, 8, C]), op=ALU.mult),
            waits=[(cv, v_lgq), CST()], inc=cv)
        r.op("scalar", lambda e: e.activation(out=a_sm[:], in_=atmp[:], func=AF.Exp),
             waits=[(cv, va)], inc=cs)
        decay_s = cs.n
        decay_v = cv.n
        emit_casts()
        pstack.close()
        tb = [cbf[0][:, 0:1024], cbf[0][:, 1024:2048],
              cbf[1][:, 0:1024], cbf[1][:, 1024:2048]]

        ltw = [Ctr(mksem(f"ltw{i}"), 16) for i in range(4)]
        lts = [Ctr(mksem(f"lts{i}"), 16) for i in range(4)]
        tjobs = []
        for ep in range(4):
            for dc in range(8):
                tjobs.append((shup_bf[ep * 512:(ep + 1) * 512, dc * 128:(dc + 1) * 128],
                              shupT_d[dc * 128:(dc + 1) * 128, ep * 512:(ep + 1) * 512],
                              512, 4, None))
            for ic in range(4):
                j = ep * 4 + ic
                tjobs.append((shdown_bf[:, j * 128:(j + 1) * 128],
                              shdownT_d[j * 128:(j + 1) * 128, :],
                              1024, 5, NE + ep if ic == 3 else None))
        for e_ in range(NE):
            for dc in range(8):
                tjobs.append((up_bf[e_][:, dc * 128:(dc + 1) * 128],
                              up_bfT[e_][dc * 128:(dc + 1) * 128, :],
                              512, 6 + 2 * e_, None))
            for ic in range(4):
                tjobs.append((down_bf[e_][:, ic * 128:(ic + 1) * 128],
                              down_bfT[e_][ic * 128:(ic + 1) * 128, :],
                              1024, 7 + 2 * e_, e_ if ic == 3 else None))

        tp_prev_st = [None] * 4
        tp_ptr = [0]

        def emit_tjobs(n):
            k0 = tp_ptr[0]
            for k in range(k0, min(k0 + n, len(tjobs))):
                src, dst, w, jidx, key = tjobs[k]
                sl = k % 4
                wl = [(jobs_done[jidx][0], jobs_done[jidx][0].n)]
                if k < 4:
                    wl += [(cast_final[0][0], cast_final[0][0].n),
                           (cast_final[1][0], cast_final[1][0].n)]
                if tp_prev_st[sl] is not None:
                    wl.append(tp_prev_st[sl])
                r.op("sync", lambda e, src=src, sl=sl, w=w: e.dma_start(
                    out=tb[sl][:, 0:w], in_=src, transpose=True),
                    waits=wl, inc=ltw[sl], selfsync=False)
                lv = ltw[sl].n
                r.op("gpsimd", lambda e, dst=dst, sl=sl, w=w: e.dma_start(
                    out=dst, in_=tb[sl][:, 0:w]),
                    waits=[(ltw[sl], lv)], inc=lts[sl], selfsync=False)
                tp_prev_st[sl] = (lts[sl], lts[sl].n)
            tp_ptr[0] = min(k0 + n, len(tjobs))

        for b_ in range(n_batches):
            r.op("gpsimd", lambda e, b_=b_: e.dma_start(out=out[b_], in_=x_in[b_]),
                 inc=pf, selfsync=False)

        with astack:
            wqkvT = sbA("wqkvT", [128, 8 * 3 * D], BF16)
            woT = sbA("woT", [128, 8 * D], BF16)
            if do_attn:
                x_tc = [sbA(f"x_tc{i}", [128, D]) for i in range(2)]
                xn_sc = sbA("xn_sc", [128, 4 * D], BF16)
                xnT_sc = [sbA(f"xnT_sc{i}", [128, 8 * SC], BF16) for i in range(2)]
                s_sb = sbA("s_sb", [128, 4 * D], BF16)
                stat = sbA("stat", [128, 8])
                sqtmp = sbA("sqtmp", [128, D])
                kstat = sbA("kstat", [128, 64])
                QT = sbA("QT", [128, 8 * SC], BF16)
                Qa = sbA("Qa", [128, 8 * SC], BF16)
                Kf = sbA("Kf", [128, D])
                Kbf = sbA("Kbf", [128, 4 * D], BF16)
                Vbf = sbA("Vbf", [128, 4 * D], BF16)
                Vw_ch = sbA("Vw_ch", [128, D], BF16)
                KT_z = sbA("KT_z", [128, H * C], BF16)
                A_bf = sbA("A_bf", [128, H * C], BF16)
                yT = sbA("yT", [128, 8 * SC], BF16)
                S_z = sbA("S_z", [128, H * DK], BF16)

                qkv_ps = [psA(f"qkv_ps{i}", [128, 512]) for i in range(2)]
                kt_ps = psA("kt_ps", [128, 1024], BF16)
                sc_ps = psA("sc_ps", [128, 8 * C])
                y_ps = psA("y_ps", [128, 8 * C])
                st_ps = psA("st_ps", [128, 8 * DK])

                free_x = [(cv, 0), (cv, 0)]
                free_xnT = [(ct, 0), (ct, 0)]
                ev_qkv = [[], []]
                st_xn_prev = 0
                st_s_prev = 0
                last_state_v = 0
                last_yevac_s = 0
                last_kt_v = 0
                last_A_v = 0
                last_Y_t = 0
                last_ksq_v = 0
                last_kln_v = 0
                v_xn_prev = 0
                ev_s = 0
                vxn_tc = [0, 0, 0, 0]
                first_A_mm = True

                r.op("vector", lambda e: e.memset(KT_z[:], 0.0), inc=cv)
                lnout = {}

                def emit_ln_tc(isc, tc):
                    nonlocal st_xn_prev, last_ksq_v, v_xn_prev
                    b, sc = divmod(isc, L // SC)
                    if True:
                        xb = (isc * 4 + tc) % 2
                        xa = x_in[b, sc * SC + tc * 128: sc * SC + (tc + 1) * 128, :]
                        r.op("sync", lambda e, xa=xa, xb=xb: e.dma_start(
                            out=x_tc[xb][:], in_=xa), waits=[free_x[xb]], inc=lxA[xb],
                            selfsync=False)
                        ld_x = (lxA[xb], lxA[xb].n)
                        s_sq = r.op("scalar", lambda e, xb=xb: e.activation(
                            out=sqtmp[:], in_=x_tc[xb][:], func=AF.Square,
                            accum_out=stat[:, 1:2]),
                            waits=[ld_x, (cv, last_ksq_v), (cv, v_xn_prev)], inc=cs)
                        s_sum = r.op("scalar", lambda e, xb=xb: e.activation(
                            out=sqtmp[:], in_=x_tc[xb][:], func=AF.Identity,
                            accum_out=stat[:, 0:1]),
                            waits=[], inc=cs)
                        r.op("scalar", lambda e: e.activation(
                            out=stat[:, 2:3], in_=stat[:, 0:1], func=AF.Identity,
                            scale=1.0 / D), inc=cs)
                        r.op("scalar", lambda e: e.activation(
                            out=stat[:, 3:4], in_=stat[:, 2:3], func=AF.Square), inc=cs)
                        s_va = r.op("scalar", lambda e: e.activation(
                            out=stat[:, 1:2], in_=stat[:, 1:2], func=AF.Identity,
                            scale=1.0 / D), inc=cs)
                        v_v2 = r.op("vector", lambda e: e.tensor_scalar(
                            out=stat[:, 1:2], in0=stat[:, 1:2], scalar1=stat[:, 3:4],
                            scalar2=LN_EPS, op0=ALU.subtract, op1=ALU.add),
                            waits=[(cs, s_va)], inc=cv)
                        s_std = r.op("scalar", lambda e: e.activation(
                            out=stat[:, 4:5], in_=stat[:, 1:2], func=AF.Sqrt),
                            waits=[(cv, v_v2)], inc=cs)
                        r.op("vector", lambda e: e.reciprocal(
                            out=stat[:, 5:6], in_=stat[:, 4:5]), waits=[(cs, s_std)], inc=cv)
                        wv = [(sxn, st_xn_prev)] if tc == 0 else []
                        v_xn = r.op("vector", lambda e, tc=tc, xb=xb: e.tensor_scalar(
                            out=xn_sc[:, tc * D:(tc + 1) * D], in0=x_tc[xb][:],
                            scalar1=stat[:, 2:3], scalar2=stat[:, 5:6],
                            op0=ALU.subtract, op1=ALU.mult), waits=wv, inc=cv)
                        free_x[xb] = (cv, v_xn)
                        v_xn_prev = v_xn
                        vxn_tc[tc] = v_xn

                def emit_ln_finish(isc):
                    nonlocal st_xn_prev
                    t0 = isc * SC
                    buf = isc % 2
                    for tc in range(4):
                        st_xn_prev = r.op("sync", lambda e, tc=tc, t0=t0: e.dma_start(
                            out=xn_dram[t0 + tc * 128: t0 + (tc + 1) * 128, :],
                            in_=xn_sc[:, tc * D:(tc + 1) * D]),
                            waits=[(cv, vxn_tc[tc])], inc=sxn, selfsync=False)
                    st_xn = st_xn_prev
                    for dc in range(8):
                        r.op("sync", lambda e, dc=dc, buf=buf, t0=t0: e.dma_start(
                            out=xnT_sc[buf][:, dc * SC:(dc + 1) * SC],
                            in_=xn_dram[t0:t0 + SC, dc * 128:(dc + 1) * 128], transpose=True),
                            waits=[(sxn, st_xn), free_xnT[buf], (sxnT, sxnT.n)],
                            inc=lxnT[buf], selfsync=False)
                    lnout[isc] = (lxnT[buf], lxnT[buf].n)
                    r.op("sync", lambda e, buf=buf, t0=t0: e.dma_start(
                        out=xnT_dram.rearrange("(dc p) t -> p dc t", p=128)[:, :, t0:t0 + SC],
                        in_=xnT_sc[buf][:, 0:8 * SC].rearrange("p (dc t) -> p dc t", t=SC)),
                        waits=[lnout[isc]], inc=sxnT, selfsync=False)

                def emit_ln(isc):
                    for tc in range(4):
                        emit_ln_tc(isc, tc)
                    emit_ln_finish(isc)

                emit_ln(0)
                ld_wqkvT_list = []
                for dc in range(8):
                    wc = (ldw0, ldw0.n) if dc < 4 else (ldw0b, ldw0b.n)
                    r.op("sync", lambda e, dc=dc: e.dma_start(
                        out=wqkvT[:, dc * 3 * D:(dc + 1) * 3 * D],
                        in_=wqkv_bf[:, dc * 128:(dc + 1) * 128], transpose=True),
                        waits=[wc] +
                              ([(cv, decay_v), (cs, decay_s)] if dc == 0 else []),
                        inc=lwT, selfsync=False)
                for dc in range(8):
                    r.op("sync", lambda e, dc=dc: e.dma_start(
                        out=woT[:, dc * D:(dc + 1) * D],
                        in_=wo_bf[:, dc * 128:(dc + 1) * 128], transpose=True),
                        waits=[(ldw0, ldw0.n)], inc=lwT, selfsync=False)
                WT = (lwT, lwT.n)

                for isc in range(NSC):
                    b, sc = divmod(isc, L // SC)
                    t0 = isc * SC
                    buf = isc % 2
                    if sc == 0:
                        last_state_v = r.op("vector", lambda e: e.memset(S_z[:], 0.0),
                                            waits=[(ct, last_Y_t)], inc=cv)
                    ld_xnT = lnout[isc]

                    if attn_stage < 2:
                        continue
                    ev_q = [0] * 8
                    for fc in range(8):
                        pp = fc % 2
                        w0 = [ld_xnT, WT, (cv, v_lgb)] + ev_qkv[pp]
                        for dc in range(8):
                            tq = r.op("tensor", lambda e, fc=fc, dc=dc, buf=buf, pp=pp: e.matmul(
                                qkv_ps[pp][:],
                                wqkvT[:, dc * 3 * D + fc * 128: dc * 3 * D + (fc + 1) * 128],
                                xnT_sc[buf][:, dc * SC:(dc + 1) * SC],
                                start=(dc == 0), stop=(dc == 7)),
                                waits=(w0 if dc == 0 else []),
                                inc=(ct if dc == 7 else None), selfsync=False)
                        ev_q[fc] = r.op("scalar", lambda e, fc=fc, pp=pp: e.activation(
                            out=QT[:, fc * SC:(fc + 1) * SC], in_=qkv_ps[pp][:], func=AF.Copy),
                            waits=[(ct, tq)], inc=cs)
                        ev_qkv[pp] = [(cs, ev_q[fc])]
                    v_qa = r.op("vector", lambda e: e.tensor_tensor(
                        out=Qa[:].rearrange("p (f c t) -> p f c t", c=4, t=C),
                        in0=QT[:].rearrange("p (f c t) -> p f c t", c=4, t=C),
                        in1=a_sm[:].rearrange("p (f t) -> p f t", t=C).unsqueeze(2)
                            .broadcast_to([128, 8, 4, C]), op=ALU.mult),
                        waits=[(cs, ev_q[7]), (cs, decay_s)], inc=cv)
                    ev_v_tc = [0] * 4
                    v_kln_tc = [0] * 4
                    for tc in range(4):
                        for nch in range(4):
                            pp = nch % 2
                            w0 = list(ev_qkv[pp])
                            for dc in range(8):
                                tkv = r.op("tensor", lambda e, tc=tc, nch=nch, dc=dc, buf=buf,
                                           pp=pp: e.matmul(
                                    qkv_ps[pp][:],
                                    xnT_sc[buf][:, dc * SC + tc * 128: dc * SC + (tc + 1) * 128],
                                    wqkvT[:, dc * 3 * D + D + nch * 512:
                                          dc * 3 * D + D + (nch + 1) * 512],
                                    start=(dc == 0), stop=(dc == 7)),
                                    waits=(w0 if dc == 0 else []),
                                    inc=(ct if dc == 7 else None), selfsync=False)
                            if nch < 2:
                                ev = r.op("scalar", lambda e, nch=nch, pp=pp: e.activation(
                                    out=Kf[:, nch * 512:(nch + 1) * 512], in_=qkv_ps[pp][:],
                                    func=AF.Copy),
                                    waits=[(ct, tkv), (cv, last_kln_v)], inc=cs)
                            else:
                                ev = r.op("scalar", lambda e, tc=tc, nch=nch, pp=pp: e.activation(
                                    out=Vbf[:, tc * D + (nch - 2) * 512:
                                            tc * D + (nch - 1) * 512],
                                    in_=qkv_ps[pp][:], func=AF.Copy),
                                    waits=[(ct, tkv)], inc=cs)
                                ev_v_tc[tc] = ev
                            ev_qkv[pp] = [(cs, ev)]
                        kvw = Kf[:].rearrange("p (h d) -> p h d", d=DK)
                        r.op("vector", lambda e, kvw=kvw: e.reduce_sum(
                            out=kstat[:, 0:16], in_=kvw, axis=AX.X),
                            waits=[(cs, cs.n)], inc=cv)
                        s_ksq = r.op("scalar", lambda e: e.activation(
                            out=sqtmp[:], in_=Kf[:], func=AF.Square),
                            waits=[(cv, last_ksq_v)], inc=cs)
                        last_ksq_v = r.op("vector", lambda e: e.reduce_sum(
                            out=kstat[:, 16:32], in_=sqtmp[:].rearrange("p (h d) -> p h d", d=DK),
                            axis=AX.X), waits=[(cs, s_ksq)], inc=cv)
                        r.op("vector", lambda e: e.tensor_scalar(
                            out=kstat[:, 32:48], in0=kstat[:, 0:16], scalar1=1.0 / DK,
                            scalar2=None, op0=ALU.mult), inc=cv)
                        r.op("vector", lambda e: e.tensor_tensor(
                            out=kstat[:, 0:16], in0=kstat[:, 32:48], in1=kstat[:, 32:48],
                            op=ALU.mult), inc=cv)
                        r.op("vector", lambda e: e.tensor_scalar(
                            out=kstat[:, 16:32], in0=kstat[:, 16:32], scalar1=1.0 / DK,
                            scalar2=LN_EPS, op0=ALU.mult, op1=ALU.add), inc=cv)
                        v_kv = r.op("vector", lambda e: e.tensor_tensor(
                            out=kstat[:, 16:32], in0=kstat[:, 16:32], in1=kstat[:, 0:16],
                            op=ALU.subtract), inc=cv)
                        s_kstd = r.op("scalar", lambda e: e.activation(
                            out=kstat[:, 16:32], in_=kstat[:, 16:32], func=AF.Sqrt),
                            waits=[(cv, v_kv)], inc=cs)
                        r.op("vector", lambda e: e.reciprocal(
                            out=kstat[:, 48:64], in_=kstat[:, 16:32]),
                            waits=[(cs, s_kstd)], inc=cv)
                        r.op("vector", lambda e, kvw=kvw: e.tensor_tensor(
                            out=kvw, in0=kvw,
                            in1=kstat[:, 32:48].unsqueeze(2).broadcast_to([128, H, DK]),
                            op=ALU.subtract), inc=cv)
                        v_kln_tc[tc] = r.op("vector", lambda e, tc=tc, kvw=kvw: e.tensor_tensor(
                            out=Kbf[:, tc * D:(tc + 1) * D].rearrange("p (h d) -> p h d", d=DK),
                            in0=kvw, in1=kstat[:, 48:64].unsqueeze(2).broadcast_to([128, H, DK]),
                            op=ALU.mult), inc=cv)
                        last_kln_v = v_kln_tc[tc]

                    if attn_stage < 3:
                        continue
                    for c in range(4):
                      if True:
                        tkt = 0
                        for f in range(8):
                            tkt = r.op("tensor", lambda e, c=c, f=f: e.transpose(
                                kt_ps[:, f * 128:(f + 1) * 128],
                                Kbf[:, c * D + f * 128: c * D + (f + 1) * 128], ident_sb[:]),
                                waits=([(cv, v_kln_tc[c]), (cv, last_kt_v),
                                        CST()] if f == 0 else []),
                                inc=(ct if f == 7 else None), selfsync=False)
                        for pb_ in range(2):
                            last_kt_v = r.op("vector", lambda e, pb_=pb_: e.tensor_copy(
                                out=KT_z[pb_ * 64:(pb_ + 1) * 64, :]
                                    .rearrange("p (j two t) -> p j two t", two=2, t=C)
                                    [:, :, pb_:pb_ + 1, :].squeeze(2),
                                in_=kt_ps[pb_ * 64:(pb_ + 1) * 64, :]
                                    .rearrange("p (j t) -> p j t", t=C)),
                                waits=[(ct, tkt)], inc=cv)
                      if attn_stage < 31:
                        continue
                      if True:
                        v_vw = r.op("vector", lambda e, c=c: e.tensor_tensor(
                            out=Vw_ch[:], in0=Vbf[:, c * D:(c + 1) * D], in1=b_full[:],
                            op=ALU.mult), waits=[(cs, ev_v_tc[c]), (cs, decay_s),
                                                 (ct, ct.n)], inc=cv)
                        if attn_stage < 32:
                            continue
                        for hh in range(2):
                            tsc = 0
                            for hi in range(8):
                                h = hh * 8 + hi
                                fc = h // 2
                                pb = (h % 2) * 64
                                tsc = r.op("tensor", lambda e, h=h, fc=fc, hi=hi, c=c:
                                           e.matmul(
                                    sc_ps[:, hi * 128:(hi + 1) * 128],
                                    KT_z[:, h * C:(h + 1) * C],
                                    QT[:, fc * SC + c * 128: fc * SC + (c + 1) * 128],
                                    start=True, stop=True),
                                    waits=([(cv, last_kt_v), (cv, last_A_v)] if hi == 0 else []),
                                    inc=(ct if hi == 7 else None), selfsync=False)
                            if attn_stage < 33:
                                continue
                            wA = [(ct, tsc), (cs, decay_s), (cv, decay_v)]
                            if not first_A_mm:
                                wA.append((ct, last_Y_t))
                            last_A_v = r.op("vector", lambda e, hh=hh: e.tensor_tensor(
                                out=A_bf[:, hh * 8 * C:(hh + 1) * 8 * C],
                                in0=sc_ps[:], in1=DT_all[:, hh * 8 * C:(hh + 1) * 8 * C],
                                op=ALU.mult), waits=wA, inc=cv)
                            first_A_mm = False
                            if attn_stage < 50:
                                continue
                            tY = 0
                            for hi in range(8):
                                h = hh * 8 + hi
                                fc = h // 2
                                pb = (h % 2) * 64
                                j = h // 2
                                tY = r.op("tensor", lambda e, h=h, c=c, pb=pb, j=j: e.matmul(
                                    y_ps[pb:pb + 64, j * 128:(j + 1) * 128],
                                    Vbf[:, c * D + h * DK: c * D + (h + 1) * DK],
                                    A_bf[:, h * C:(h + 1) * C], start=True, stop=False),
                                    waits=([(cv, last_A_v), (cs, last_yevac_s),
                                            (cv, last_state_v)] if hi == 0 else []),
                                    inc=None, selfsync=False)
                                tY = r.op("tensor", lambda e, h=h, c=c, pb=pb, j=j, fc=fc:
                                          e.matmul(
                                    y_ps[pb:pb + 64, j * 128:(j + 1) * 128],
                                    S_z[:, h * DK:(h + 1) * DK],
                                    Qa[:, fc * SC + c * 128: fc * SC + (c + 1) * 128],
                                    start=False, stop=True),
                                    inc=(ct if hi == 7 else None), selfsync=False)
                            last_Y_t = tY
                        if attn_stage < 50:
                            continue
                        last_yevac_s = r.op("scalar", lambda e, c=c: e.activation(
                            out=yT[:, c * 1024:(c + 1) * 1024],
                            in_=y_ps[:], func=AF.Copy),
                            waits=[(ct, last_Y_t)], inc=cs)
                        if attn_stage < 60:
                            continue
                        tst = 0
                        for h in range(H):
                            pb = (h % 2) * 64
                            j = h // 2
                            r.op("tensor", lambda e, h=h, c=c, pb=pb, j=j: e.matmul(
                                st_ps[pb:pb + 64, j * DK:(j + 1) * DK],
                                Kbf[:, c * D + h * DK:c * D + (h + 1) * DK],
                                Vw_ch[:, h * DK:(h + 1) * DK],
                                start=True, stop=False),
                                waits=([(cv, last_state_v), (cv, v_vw)] if h == 0 else []),
                                inc=None, selfsync=False)
                            tst = r.op("tensor", lambda e, h=h, pb=pb, j=j: e.matmul(
                                st_ps[pb:pb + 64, j * DK:(j + 1) * DK],
                                gI_z[:, h * DK:(h + 1) * DK],
                                S_z[:, h * DK:(h + 1) * DK],
                                start=False, stop=True),
                                inc=(ct if h == H - 1 else None), selfsync=False)
                        for pb_ in range(2):
                            last_state_v = r.op("vector", lambda e, pb_=pb_: e.tensor_copy(
                                out=S_z[pb_ * 64:(pb_ + 1) * 64, :]
                                    .rearrange("p (j two d) -> p j two d", two=2, d=DK)
                                    [:, :, pb_:pb_ + 1, :].squeeze(2),
                                in_=st_ps[pb_ * 64:(pb_ + 1) * 64, :]
                                    .rearrange("p (j d) -> p j d", d=DK)),
                                waits=([(ct, tst), (ct, last_Y_t)] if pb_ == 0 else []), inc=cv)
                        if isc + 1 < NSC:
                            emit_ln_tc(isc + 1, c)
                            if c == 3:
                                emit_ln_finish(isc + 1)
                        if attn_stage >= 70:
                            tc = c
                            pp = tc % 2
                            for n in range(2):
                                w0 = [(cs, last_yevac_s), WT] + ev_qkv[pp]
                                if tc == 0 and n == 0:
                                    w0.append((ss, st_s_prev))
                                for j in range(8):
                                    tw = r.op("tensor", lambda e, tc=tc, n=n, j=j, pp=pp:
                                              e.matmul(
                                        qkv_ps[pp][:],
                                        yT[:, tc * 1024 + j * 128: tc * 1024 + (j + 1) * 128],
                                        woT[:, j * D + n * 512: j * D + (n + 1) * 512],
                                        start=(j == 0), stop=(j == 7)),
                                        waits=(w0 if j == 0 else []),
                                        inc=(ct if j == 7 else None), selfsync=False)
                                ev_s = r.op("scalar", lambda e, tc=tc, n=n, pp=pp:
                                            e.activation(
                                    out=s_sb[:, tc * D + n * 512: tc * D + (n + 1) * 512],
                                    in_=qkv_ps[pp][:], func=AF.Copy),
                                    waits=[(ct, tw)], inc=cs)
                                ev_qkv[pp] = [(cs, ev_s)]
                        if isc >= 7:
                            emit_tjobs(12)

                    if attn_stage < 70:
                        continue
                    for tc in range(4):
                        st_s_prev = r.op("sync", lambda e, tc=tc, t0=t0: e.dma_start(
                            out=s_dram[t0 + tc * 128: t0 + (tc + 1) * 128, :],
                            in_=s_sb[:, tc * D:(tc + 1) * D]),
                            waits=[(cs, ev_s)], inc=ss, selfsync=False)
                    free_xnT[buf] = (ct, ct.n)

        emit_tjobs(len(tjobs))
        phA = dict(t=ct.n, v=cv.n, s=cs.n, sxn=sxn.n, ss=ss.n)

        if do_ffn:
            bstack = contextlib.ExitStack()
            sbB, psB = mk_alloc(bstack)
            with bstack:
                NEE = NE + 4
                xnT_g = [sbB(f"xnT_g{i}", [128, 8 * GT], BF16) for i in range(2)]
                acc2 = [sbB(f"acc{i}", [128, 8 * D]) for i in range(2)]
                upT = [sbB(f"upT{i}", [128, 8 * 512], BF16) for i in range(2)]
                dnT = [sbB(f"dnT{i}", [128, 4 * D], BF16) for i in range(2)]
                hr = sbB("hr", [128, 2048], BF16)
                hp_sb = [sbB(f"hp_sb{i}", [128, 2048], BF16) for i in range(2)]
                gateT = sbB("gateT", [128, 8 * NE], BF16)
                ebias_b = sbB("ebias_b", [128, NE])
                ebrow = sbB("ebrow", [1, NE])
                comb = sbB("comb", [128, 8 * NE])
                rsc_all = sbB("rsc_all", [128, 8 * NE])
                rwork = sbB("rwork", [128, 8 * NE])
                rtmp = sbB("rtmp", [128, 8 * NE])
                rtm2 = sbB("rtm2", [128, 8 * NE])
                pr6 = sbB("pr6", [128, 6 * 64])
                gsco = sbB("gsco", [128, 3 * 64])
                gtmp = sbB("gtmp", [128, 64])
                mx8x = sbB("mx8x", [128, 64])
                mrep4x = sbB("mrep4x", [128, 64])
                mrep6x = sbB("mrep6x", [128, 64])
                rden = sbB("rden", [128, 8])
                gs8 = sbB("gs8", [128, 8 * D], BF16)
                fence_sb = sbB("fence_sb", [1, 8])
                s8 = sbB("s8", [128, 8 * D], BF16)


                h_ps = [psB(f"h_ps{i}", [128, 1024]) for i in range(2)]
                dn_ps = [psB(f"dn_ps{i}", [128, 1024]) for i in range(2)]

                cg = r_cg
                barrier = [(ct, phA["t"]), (cv, phA["v"]), (cs, phA["s"]),
                           (sxn, phA["sxn"]), (ss, phA["ss"]), (lwT, lwT.n)]

                for dc in range(8):
                    r.op("sync", lambda e, dc=dc: e.dma_start(
                        out=gateT[:, dc * NE:(dc + 1) * NE],
                        in_=gate_bf[:, dc * 128:(dc + 1) * 128], transpose=True),
                        waits=[(ldw1, ldw1.n)] + (barrier if dc == 0 else []),
                        inc=lgT, selfsync=False)
                r.op("sync", lambda e: e.dma_start(out=ebrow[:], in_=e_bias),
                     inc=lgT, selfsync=False)
                ld_gateT = ld_eb = (lgT, lgT.n)
                t_eb = r.op("tensor", lambda e: e.matmul(
                    dn_ps[0][:, 0:NE], ones_row[:], ebrow[:], start=True, stop=True),
                    waits=[ld_eb] + barrier, inc=ct, selfsync=False)
                v_eb = r.op("vector", lambda e: e.tensor_copy(
                    out=ebias_b[:], in_=dn_ps[0][:, 0:NE]),
                    waits=[(ct, t_eb)] + barrier, inc=cv)
                r.op("vector", lambda e: e.memset(mrep4x[:], 1e30), inc=cv)
                r.op("vector", lambda e: e.memset(mrep6x[:], 1e30), inc=cv)
                s_bar = r.op("scalar", lambda e: e.activation(
                    out=ebias_b[:, 0:1], in_=ebias_b[:, 0:1], func=AF.Copy),
                    waits=[(cv, v_eb)] + barrier, inc=cs)
                r.op("gpsimd", lambda e: e.memset(fence_sb[:], 0.0),
                     waits=barrier, inc=cg, selfsync=False)

                free_xnTg = [(ct, 0), (ct, 0)]
                free_w = [(ct, 0), (ct, 0)]

                ld_xnTg = [0, 0]
                ld_we = [0, 0]
                last_stt_pp = [(cv, v_eb), (cv, 0)]
                last_relu_s = [0, 0]
                last_hp_v = [0, 0]
                dn_done = [(ct, 0), (ct, 0)]
                hp_ready = [(cv, 0), (cv, 0)]
                router_prev = 0
                s_g_last = 0

                sxnT_all = sxnT.n

                def load_group(g):
                    gb = g % 2
                    r.op("sync", lambda e, g=g, gb=gb: e.dma_start(
                        out=xnT_g[gb][:, 0:8 * GT].rearrange("p (dc t) -> p dc t", t=GT),
                        in_=xnT_dram.rearrange("(dc p) t -> p dc t", p=128)
                            [:, :, g * GT:(g + 1) * GT]),
                        waits=[(sxnT, sxnT_all), free_xnTg[gb]],
                        inc=lxg[gb], selfsync=False)
                    ld_xnTg[gb] = (lxg[gb], lxg[gb].n)

                def load_expert(e_, slot):
                    wtr = (lts[0], lts[0].n)
                    wtr2 = [(lts[i], lts[i].n) for i in range(1, 4)]
                    if e_ < NE:
                        r.op("sync", lambda e, e_=e_, slot=slot: e.dma_start(
                            out=upT[slot][:, 0:8 * 512].rearrange("p (dc j) -> p dc j", j=512),
                            in_=up_bfT[e_].rearrange("(dc p) j -> p dc j", p=128)),
                            waits=[wtr] + wtr2 + [free_w[slot]],
                            inc=lwe[slot], selfsync=False)
                        r.op("sync", lambda e, e_=e_, slot=slot: e.dma_start(
                            out=dnT[slot][:, 0:4 * D].rearrange("p (ic h) -> p ic h", h=D),
                            in_=down_bfT[e_].rearrange("(ic p) h -> p ic h", p=128)),
                            waits=[], inc=lwe[slot], selfsync=False)
                    else:
                        ep = e_ - NE
                        r.op("sync", lambda e, ep=ep, slot=slot: e.dma_start(
                            out=upT[slot][:, 0:8 * 512].rearrange("p (dc j) -> p dc j", j=512),
                            in_=shupT_d.rearrange("(dc p) si -> p dc si", p=128)
                                [:, :, ep * 512:(ep + 1) * 512]),
                            waits=[wtr] + wtr2 + [free_w[slot]],
                            inc=lwe[slot], selfsync=False)
                        r.op("sync", lambda e, ep=ep, slot=slot: e.dma_start(
                            out=dnT[slot][:, 0:4 * D].rearrange("p (ic h) -> p ic h", h=D),
                            in_=shdownT_d[ep * 512:(ep + 1) * 512, :]
                                .rearrange("(ic p) h -> p ic h", p=128)),
                            waits=[], inc=lwe[slot], selfsync=False)
                    ld_we[slot] = (lwe[slot], lwe[slot].n)

                load_group(0)
                if NGRP > 1:
                    load_group(1)
                load_expert(NE, 0)
                load_expert(NE + 1, 1)

                fin_done = [(cv, 0), (cv, 0)]
                pend_fin = None
                free_s8 = (cv, 0)
                free_gs8 = (cs, 0)
                def vfin_cur():
                    return max(last_stt_pp[0][1], last_stt_pp[1][1])

                def emit_finalize(gf):
                    nonlocal fin_done, free_gs8
                    g_, vfin = gf
                    acc = acc2[g_ % 2]
                    for i8 in range(8):
                        ls = r.op("gpsimd", lambda e, g_=g_, i8=i8: e.dma_start(
                            out=s8[:, i8 * D:(i8 + 1) * D],
                            in_=s_dram[g_ * GT + i8 * 128: g_ * GT + (i8 + 1) * 128, :]),
                            waits=([(ss, phA["ss"]), free_s8] if i8 == 0 else []),
                            inc=lsl[0], selfsync=False)
                    ls = (lsl[0], lsl[0].n)
                    s_all = r.op("scalar", lambda e, acc=acc: e.activation(
                        out=gs8[:], in_=acc[:], func=AF.Sigmoid),
                        waits=[(cv, vfin), free_gs8], inc=cs)
                    v_g8 = r.op("vector", lambda e, acc=acc: e.tensor_tensor(
                        out=gs8[:], in0=acc[:], in1=gs8[:], op=ALU.mult),
                        waits=[(cs, s_all)], inc=cv)
                    fin_done[g_ % 2] = (cv, v_g8)
                    v_sg = r.op("vector", lambda e: e.tensor_tensor(
                        out=gs8[:], in0=s8[:], in1=gs8[:], op=ALU.mult),
                        waits=[ls], inc=cv)
                    free_s8_v = (cv, v_sg)
                    for idx in range(8):
                        row0 = g_ * GT + idx * 128
                        b = row0 // L
                        l0 = row0 % L
                        r.op("gpsimd", lambda e, b=b, l0=l0, idx=idx: e.dma_start(
                            out=out[b, l0:l0 + 128, :],
                            in_=gs8[:, idx * D:(idx + 1) * D], accum_op=ALU.add),
                            waits=([(cv, v_sg), (pf, pf.n)] if idx == 0 else []),
                            inc=sto[0], selfsync=False)
                    free_gs8 = (sto[0], sto[0].n)
                    return free_s8_v


                for g in range(NGRP):
                    gb = g % 2
                    acc = acc2[g % 2]
                    v_z = 0
                    for i in range(8):
                        v_z = r.op("vector", lambda e, i=i, acc=acc: e.memset(
                            acc[:, i * D:(i + 1) * D], 0.0),
                            waits=([fin_done[g % 2]] if i == 0 else []), inc=cv)
                    s_sig2 = 0
                    for j2 in range(2):
                        tl = 0
                        for tl4 in range(4):
                            w0 = ([ld_xnTg[gb], ld_gateT,
                                   (cs, last_relu_s[j2]), (cg, last_hp_v[j2])]
                                  if tl4 == 0 else [])
                            for dc in range(8):
                                tl = r.op("tensor", lambda e, j2=j2, tl4=tl4, dc=dc, gb=gb:
                                          e.matmul(
                                    h_ps[j2][:, tl4 * 32:(tl4 + 1) * 32],
                                    xnT_g[gb][:, dc * GT + (j2 * 4 + tl4) * 128:
                                               dc * GT + (j2 * 4 + tl4 + 1) * 128],
                                    gateT[:, dc * NE:(dc + 1) * NE],
                                    start=(dc == 0), stop=(dc == 7)),
                                    waits=(w0 if dc == 0 else []),
                                    inc=(ct if (tl4 == 3 and dc == 7) else None),
                                    selfsync=False)
                        s_sig2 = r.op("scalar", lambda e, j2=j2: e.activation(
                            out=rsc_all[:, j2 * 128:(j2 + 1) * 128],
                            in_=h_ps[j2][:, 0:128], func=AF.Sigmoid),
                            waits=[(ct, tl), (cv, router_prev)], inc=cs)
                        last_relu_s[j2] = s_sig2

                    def vst(fn, waits=()):
                        return r.op("vector", fn, waits=waits, inc=cv, selfsync=False)
                    rw4 = rwork[:].rearrange("p (i g m) -> p i g m", g=8, m=4)
                    st = vst(lambda e: e.tensor_tensor(
                        out=rwork[:].rearrange("p (i e) -> p i e", e=NE),
                        in0=rsc_all[:].rearrange("p (i e) -> p i e", e=NE),
                        in1=ebias_b[:].unsqueeze(1).broadcast_to([128, 8, NE]),
                        op=ALU.add), waits=[(cs, s_sig2), (cv, v_eb), (cv, cv.n)])
                    pairs = [(0, 1), (2, 3), (0, 2), (1, 3), (0, 3), (1, 2)]
                    for pi, (pa, pb_) in enumerate(pairs):
                        st2 = vst(lambda e, pa=pa, pb_=pb_, pi=pi: e.tensor_tensor(
                            out=pr6[:, pi * 64:(pi + 1) * 64],
                            in0=rw4[:, :, :, pa:pa + 1].squeeze(3),
                            in1=rw4[:, :, :, pb_:pb_ + 1].squeeze(3), op=ALU.add),
                            waits=([(cv, st)] if pi == 0 else []))
                    st3 = None
                    for (o, a, b_) in ((0, 0, 1), (1, 2, 3), (2, 4, 5)):
                        st3 = vst(lambda e, o=o, a=a, b_=b_: e.tensor_tensor(
                            out=gsco[:, o * 64:(o + 1) * 64], in0=pr6[:, a * 64:(a + 1) * 64],
                            in1=pr6[:, b_ * 64:(b_ + 1) * 64], op=ALU.max),
                            waits=([(cv, st2)] if o == 0 else []))
                    st4 = vst(lambda e: e.tensor_tensor(
                        out=gsco[:, 0:64], in0=gsco[:, 0:64], in1=gsco[:, 64:128],
                        op=ALU.max), waits=[(cv, st3)])
                    st5 = vst(lambda e: e.tensor_tensor(
                        out=gsco[:, 0:64], in0=gsco[:, 0:64], in1=gsco[:, 128:192],
                        op=ALU.max), waits=[(cv, st4)])
                    st6 = None
                    for s_ in range(8):
                        st6 = vst(lambda e, s_=s_: e.max(
                            out=mx8x[:, s_ * 8:(s_ + 1) * 8], in_=gsco[:, s_ * 8:(s_ + 1) * 8]),
                            waits=([(cv, st5)] if s_ == 0 else []))
                    st7 = None
                    for s_ in range(8):
                        st7 = vst(lambda e, s_=s_: e.tensor_copy(
                            out=mrep4x[:, s_ * 8:s_ * 8 + 4], in_=mx8x[:, s_ * 8:s_ * 8 + 4]),
                            waits=([(cv, st6)] if s_ == 0 else []))
                    st8 = None
                    for s_ in range(8):
                        st8 = vst(lambda e, s_=s_: e.match_replace(
                            out=gtmp[:, s_ * 8:(s_ + 1) * 8], in_to_replace=mrep4x[:, s_ * 8:(s_ + 1) * 8],
                            in_values=gsco[:, s_ * 8:(s_ + 1) * 8], imm_value=-1e30),
                            waits=([(cv, st7)] if s_ == 0 else []))
                    st9 = vst(lambda e: e.tensor_scalar(
                        out=gtmp[:], in0=gtmp[:], scalar1=-1e29, scalar2=None,
                        op0=ALU.is_lt), waits=[(cv, st8)])
                    st10 = vst(lambda e: e.tensor_tensor(
                        out=rtmp[:].rearrange("p (i g m) -> p i g m", g=8, m=4),
                        in0=rw4,
                        in1=gtmp[:].rearrange("p (i g) -> p i g", g=8).unsqueeze(3)
                            .broadcast_to([128, 8, 8, 4]), op=ALU.mult),
                        waits=[(cv, st9)])
                    st11 = None
                    for s_ in range(8):
                        st11 = vst(lambda e, s_=s_: e.max(
                            out=mx8x[:, s_ * 8:(s_ + 1) * 8], in_=rtmp[:, s_ * 32:(s_ + 1) * 32]),
                            waits=([(cv, st10)] if s_ == 0 else []))
                    st12 = None
                    for s_ in range(8):
                        st12 = vst(lambda e, s_=s_: e.tensor_copy(
                            out=mrep6x[:, s_ * 8:s_ * 8 + 6], in_=mx8x[:, s_ * 8:s_ * 8 + 6]),
                            waits=([(cv, st11)] if s_ == 0 else []))
                    st13 = None
                    for s_ in range(8):
                        st13 = vst(lambda e, s_=s_: e.match_replace(
                            out=rtm2[:, s_ * 32:(s_ + 1) * 32], in_to_replace=mrep6x[:, s_ * 8:(s_ + 1) * 8],
                            in_values=rtmp[:, s_ * 32:(s_ + 1) * 32], imm_value=-1e30),
                            waits=([(cv, st12)] if s_ == 0 else []))
                    st14 = vst(lambda e: e.tensor_scalar(
                        out=rtm2[:], in0=rtm2[:], scalar1=-1e29, scalar2=None,
                        op0=ALU.is_lt), waits=[(cv, st13)])
                    st15 = vst(lambda e: e.tensor_tensor(
                        out=rwork[:], in0=rtm2[:], in1=rsc_all[:], op=ALU.mult),
                        waits=[(cv, st14)])
                    st16 = vst(lambda e: e.reduce_sum(
                        out=rden[:], in_=rwork[:].rearrange("p (i e) -> p i e", e=NE),
                        axis=AX.X), waits=[(cv, st15)])
                    st17 = vst(lambda e: e.tensor_scalar(
                        out=rden[:], in0=rden[:], scalar1=1e-20,
                        scalar2=1.0 / ROUTED_SCALE, op0=ALU.add, op1=ALU.mult),
                        waits=[(cv, st16)])
                    st18 = vst(lambda e: e.reciprocal(out=rden[:], in_=rden[:]),
                               waits=[(cv, st17)])
                    v_cmb_last = vst(lambda e: e.tensor_tensor(
                        out=comb[:].rearrange("p (i e) -> p i e", e=NE),
                        in0=rwork[:].rearrange("p (i e) -> p i e", e=NE),
                        in1=rden[:].unsqueeze(2).broadcast_to([128, 8, NE]),
                        op=ALU.mult), waits=[(cv, st18)])
                    router_prev = v_cmb_last
                    v_router = v_cmb_last

                    NU = GT // 512
                    eorder = list(range(NE, NEE)) + list(range(NE))
                    units = [(e_, u) for e_ in eorder for u in range(NU)]

                    def emit_up(k):
                        e_, u = units[k]
                        slot = (k // NU) % 2
                        kb = k % 2
                        for hp in range(2):
                            tu = 0
                            for ic2 in range(2):
                                ic = hp * 2 + ic2
                                w0 = ([ld_we[slot], ld_xnTg[gb],
                                       (cs, last_relu_s[hp])]
                                      if ic2 == 0 else [])
                                for dc in range(8):
                                    tu = r.op("tensor", lambda e, ic=ic, dc=dc, u=u, gb=gb,
                                              slot=slot, hp=hp, ic2=ic2: e.matmul(
                                        h_ps[hp][:, ic2 * 512:(ic2 + 1) * 512],
                                        upT[slot][:, dc * 512 + ic * 128:
                                                  dc * 512 + (ic + 1) * 128],
                                        xnT_g[gb][:, dc * GT + u * 512:
                                                  dc * GT + (u + 1) * 512],
                                        start=(dc == 0), stop=(dc == 7)),
                                        waits=(w0 if dc == 0 else []),
                                        inc=(ct if (ic2 == 1 and dc == 7) else None),
                                        selfsync=False)
                            last_relu_s[hp] = r.op("scalar", lambda e, hp=hp: e.activation(
                                out=hr[:, hp * 1024:(hp + 1) * 1024], in_=h_ps[hp][:],
                                func=AF.Relu),
                                waits=[(ct, tu), (cg, last_hp_v[hp])], inc=cs)
                            last_hp_v[hp] = r.op("gpsimd", lambda e, hp=hp, kb=kb:
                                                 e.tensor_tensor(
                                out=hp_sb[kb][:, hp * 1024:(hp + 1) * 1024],
                                in0=hr[:, hp * 1024:(hp + 1) * 1024],
                                in1=hr[:, hp * 1024:(hp + 1) * 1024], op=ALU.mult),
                                waits=[(cs, last_relu_s[hp]), dn_done[kb]],
                                inc=cg, selfsync=True)
                        hp_ready[kb] = (cg, last_hp_v[1])

                    def emit_dn(k):
                        e_, u = units[k]
                        slot = (k // NU) % 2
                        kb = k % 2
                        last_td = 0
                        for tcl in range(4):
                            pp = tcl % 2
                            td = 0
                            w0 = [hp_ready[kb], last_stt_pp[pp]]
                            for n in range(2):
                                for ic in range(4):
                                    td = r.op("tensor", lambda e, tcl=tcl, n=n, ic=ic,
                                              slot=slot, pp=pp, kb=kb: e.matmul(
                                        dn_ps[pp][:, n * 512:(n + 1) * 512],
                                        hp_sb[kb][:, ic * 512 + tcl * 128:
                                              ic * 512 + (tcl + 1) * 128],
                                        dnT[slot][:, ic * D + n * 512:
                                                 ic * D + (n + 1) * 512],
                                        start=(ic == 0), stop=(ic == 3)),
                                        waits=(w0 if (n == 0 and ic == 0) else []),
                                        inc=(ct if (n == 1 and ic == 3) else None),
                                        selfsync=False)
                            last_td = td
                            idx = u * 4 + tcl
                            acc_ = acc2[g % 2]
                            if e_ < NE:
                                vv = r.op("vector", lambda e, idx=idx, e_=e_, pp=pp, acc_=acc_:
                                          e.scalar_tensor_tensor(
                                    out=acc_[:, idx * D:(idx + 1) * D], in0=dn_ps[pp][:],
                                    scalar=comb[:, idx * NE + e_: idx * NE + e_ + 1],
                                    in1=acc_[:, idx * D:(idx + 1) * D],
                                    op0=ALU.mult, op1=ALU.add),
                                    waits=[(ct, td), (cv, v_router), (cv, v_z)], inc=cv)
                            else:
                                vv = r.op("vector", lambda e, idx=idx, pp=pp, acc_=acc_:
                                          e.tensor_tensor(
                                    out=acc_[:, idx * D:(idx + 1) * D], in0=dn_ps[pp][:],
                                    in1=acc_[:, idx * D:(idx + 1) * D], op=ALU.add),
                                    waits=[(ct, td), (cv, v_z)], inc=cv)
                            last_stt_pp[pp] = (cv, vv)
                        dn_done[kb] = (ct, last_td)
                        if u == NU - 1:
                            free_w[slot] = (ct, last_td)
                            pos = k // NU
                            nxt = pos + 2
                            if nxt < NEE:
                                load_expert(eorder[nxt], nxt % 2)
                            elif g + 1 < NGRP:
                                load_expert(eorder[nxt - NEE], (nxt - NEE) % 2)

                    emit_up(0)
                    for k in range(len(units)):
                        if k + 1 < len(units):
                            emit_up(k + 1)
                        emit_dn(k)
                        if k == 3 and pend_fin is not None:
                            free_s8 = emit_finalize(pend_fin)
                            pend_fin = None
                    free_xnTg[gb] = (ct, ct.n)
                    if g + 2 < NGRP:
                        load_group(g + 2)
                    pend_fin = (g, vfin_cur())

                if pend_fin is not None:
                    free_s8 = emit_finalize(pend_fin)
                    pend_fin = None

        final_waits = [(sto[0], sto[0].n), (sto[1], sto[1].n)]
        for c_ in [ldw0, ldw0b, ldw1, lcst, lwT, lxA[0], lxA[1], lxnT[0], lxnT[1],
                   sxn, sxnT, ss, lgT, lxg[0], lxg[1], lwe[0], lwe[1], lxl[0], lxl[1],
                   lsl[0], lsl[1], lcw[0], lcw[1], lts[0], lts[1], lts[2],
                   lts[3], ltw[0], ltw[1], ltw[2], ltw[3], pf]:
            if c_.n > 0:
                final_waits.append((c_, c_.n))
        r.op("sync", lambda e: None, waits=final_waits, selfsync=False)

        block = ostack.enter_context(nc.Block())
        r.flush(block)

    return nc


def split_inputs(inp, n_cores=8):
    xs = np.ascontiguousarray(inp["L0_attn_x"])
    nb = xs.shape[0] // n_cores
    hc = host_consts()
    maps = []
    for i in range(n_cores):
        m = dict(
            x=np.ascontiguousarray(xs[i * nb:(i + 1) * nb]),
            w_qkv=inp["L0_attn_W_qkv"],
            w_o=inp["L0_attn_W_o"],
            log_gamma=np.asarray(inp["L0_attn_log_gamma"]).reshape(1, H),
            gate_w=inp["L0_ffn_gate_w"],
            e_bias=np.asarray(inp["L0_ffn_e_bias"]).reshape(1, NE),
            up_w=inp["L0_ffn_up_w"],
            down_w=inp["L0_ffn_down_w"],
            shup_w=inp["L0_ffn_shared_up_w"],
            shdown_w=inp["L0_ffn_shared_down_w"],
        )
        m.update(hc)
        maps.append({k: np.ascontiguousarray(np.asarray(v)) for k, v in m.items()})
    return maps


_NC_CACHE = {}


def kernel(**inputs):
    n_cores = 8
    xs = np.ascontiguousarray(np.asarray(inputs["L0_attn_x"], dtype=np.float32))
    nb = xs.shape[0] // n_cores
    if "nc" not in _NC_CACHE:
        _NC_CACHE["nc"] = build(n_batches=nb)
    nc = _NC_CACHE["nc"]
    maps = split_inputs(inputs, n_cores=n_cores)
    from concourse.bass_utils import run_bass_kernel_spmd
    res = run_bass_kernel_spmd(nc, maps, core_ids=list(range(n_cores)))
    out = np.concatenate([np.asarray(r["out"]) for r in res.results], axis=0)
    return out
